# version 12 (final)
# baseline (speedup 1.0000x reference)
import contextlib
import numpy as np
import concourse.bass as bass
import concourse.mybir as mybir
from concourse.bass_utils import run_bass_kernel_spmd

F32 = mybir.dt.float32
BF16 = mybir.dt.bfloat16
I32 = mybir.dt.int32
AF = mybir.ActivationFunctionType
ALU = mybir.AluOpType
AX = mybir.AxisListType
ENGS = ("tensor", "vector", "scalar", "gpsimd", "sync")

class Op:
    __slots__ = ("eng", "builder", "deps", "inc", "val", "tag", "pre_inced")

    def __init__(self, eng, builder, deps, inc, val, tag):
        self.eng, self.builder, self.deps, self.inc, self.val, self.tag = (
            eng, builder, deps, inc, val, tag)


class K:
    def __init__(self, nc):
        self.nc = nc
        self.ops = []
        self.semval = {e: 0 for e in ENGS}
        self.last = None
        self.sems = {}

    def op(self, eng, builder, after="chain", inc=1, tag="", pre_inced=False):
        if after == "chain":
            deps = [self.last] if self.last is not None else []
            if deps and deps[0][0] == eng and eng == "tensor":
                deps = []
        elif after is None:
            deps = []
        else:
            deps = [t for t in after if t is not None]
        extra = getattr(self, "extra_next_deps", None)
        if extra:
            deps = deps + [t for t in extra if t is not None]
            self.extra_next_deps = None
        self.semval[eng] += inc
        o = Op(eng, builder, deps, inc, self.semval[eng], tag)
        o.pre_inced = pre_inced
        self.ops.append(o)
        tok = (eng, self.semval[eng])
        self.last = tok
        return tok

    def dma(self, eng, out, in_, after="chain", tag="", **kw):
        return self.op(eng, lambda ns: ns.dma_start(out=out, in_=in_, **kw),
                       after=after, inc=16, tag=tag)

    def now(self, eng):
        return (eng, self.semval[eng])

    def all_now(self):
        return [(e, self.semval[e]) for e in ENGS if self.semval[e] > 0]

    def emit(self, block):
        per = {e: [] for e in ENGS}
        for o in self.ops:
            per[o.eng].append(o)
        water = {e: {p: 0 for p in ENGS} for e in ENGS}

        def mk(ename):
            def body(ns):
                w = water[ename]
                for o in per[ename]:
                    for (pe, pv) in o.deps:
                        if pe == ename == "tensor":
                            continue
                        if pv > w[pe]:
                            ns.wait_ge(self.sems[pe], pv)
                            w[pe] = pv
                    inst = o.builder(ns)
                    if getattr(o, "pre_inced", False):
                        pass
                    elif inst is not None:
                        inst.then_inc(self.sems[ename], o.inc)
                    else:
                        ns.nop().then_inc(self.sems[ename], o.inc)
            return body

        for ename in ENGS:
            if per[ename]:
                getattr(block, ename)(mk(ename))



def make_iota_diff(k, out):
    P, F = out.shape
    k.op("gpsimd", lambda g: g.iota(out, pattern=[[-1, F]], base=0,
                                    channel_multiplier=1), tag="iota")


def make_identity(k, out, val=1.0):
    P, F = out.shape
    k.op("gpsimd", lambda g: g.memset(out, val))
    k.op("gpsimd", lambda g: g.affine_select(
        out=out, in_=out, compare_op=ALU.is_equal, fill=0.0,
        base=0, pattern=[[-1, F]], channel_multiplier=1))


def mm_acc(k, psum, lhsT_tiles, rhs_tiles, after="chain", tag="mm"):
    n = len(lhsT_tiles)
    toks = []
    for i, (lt, rt) in enumerate(zip(lhsT_tiles, rhs_tiles)):
        toks.append(k.op("tensor", (lambda lt=lt, rt=rt, s=(i == 0), e=(i == n - 1):
                                    (lambda t: t.matmul(psum, lt, rt, start=s, stop=e)))(),
                         after=(after if i == 0 else "chain"), tag=tag))
    return toks[-1]


NT = 4096
D = 1024


class Ctx:
    def __init__(self, nc):
        self.nc = nc
        self.k = K(nc)
        self.st = contextlib.ExitStack()
        self.uid = 0
        self._psum = []
        self._psum_i = 0

    def setup(self):
        nc, st = self.nc, self.st
        for e in ENGS:
            self.k.sems[e] = st.enter_context(nc.semaphore("sem_" + e))
        for i in range(6):
            self._psum.append(st.enter_context(
                nc.psum_tensor(f"psb{i}", [128, 512], F32)))
        self.ps_tr = st.enter_context(nc.psum_tensor("pstr", [128, 256], BF16))
        self.ps_aux = st.enter_context(nc.psum_tensor("psaux", [128, 512], F32))
        self.dbgt = self.sb("dbgt", [128, 512], F32)
        self.ones_col = self.sb("ones_col", [128, 1], F32)
        self.ones_col_bf = self.sb("ones_col_bf", [128, 1], BF16)
        self.ones_row = self.sb("ones_row", [1, 128], F32)
        self.ident = self.sb("ident", [128, 128], BF16)
        self.ident_f = self.sb("ident_f", [128, 128], F32)
        k = self.k
        k.op("gpsimd", lambda g: g.memset(self.ones_col[:, :], 1.0), after=None)
        k.op("gpsimd", lambda g: g.memset(self.ones_col_bf[:, :], 1.0))
        k.op("gpsimd", lambda g: g.memset(self.ones_row[:, :], 1.0))
        for idt in (self.ident, self.ident_f):
            k.op("gpsimd", (lambda idt=idt: (lambda g: g.memset(idt[:, :], 1.0)))())
            k.op("gpsimd", (lambda idt=idt: (lambda g: g.affine_select(
                out=idt[:, :], in_=idt[:, :], compare_op=ALU.is_equal,
                fill=0.0, base=0, pattern=[[-1, 128]], channel_multiplier=1)))())

    def name(self, p):
        self.uid += 1
        return f"{p}_{self.uid}"

    def sb(self, nm, shp, dt=F32, stack=None):
        return (stack or self.st).enter_context(
            self.nc.sbuf_tensor(self.name(nm), shp, dt))

    def bank(self):
        self._psum_i = (self._psum_i + 1) % len(self._psum)
        return self._psum[self._psum_i]

    def fence(self):
        tok = self.k.op("vector", lambda v: v.engine_nop(),
                        after=self.k.all_now())
        if not hasattr(self, "bank_read"):
            self.bank_read = {}
        for b in self._psum:
            self.bank_read[id(b)] = tok
        self.bank_read[id(self.ps_aux)] = tok
        return tok



def bcast_row(ctx, row_ap, out_sb, N, P=128):
    k = ctx.k
    for n0 in range(0, N, 512):
        nw = min(512, N - n0)
        b = ctx.bank()
        k.op("tensor", (lambda b=b, n0=n0, nw=nw: (lambda t: t.matmul(
            b[0:P, 0:nw], ctx.ones_row[0:1, 0:P], row_ap[0:1, n0:n0 + nw],
            start=True, stop=True)))())
        k.op("vector", (lambda b=b, n0=n0, nw=nw: (lambda v: v.tensor_copy(
            out_sb[0:P, n0:n0 + nw], b[0:P, 0:nw])))())
    ctx.fence()


def dma_w(ctx, dst_sb, src, kg, mt, m0, mw, eng="gpsimd", after="chain"):
    sv = src[0:kg * 128, :].rearrange("(g p) m -> p g m", p=128)[:, :, m0:m0 + mw]
    dv = dst_sb[:, 0:kg * mt].rearrange("p (g m) -> p g m", m=mt)[:, :, 0:mw]
    return ctx.k.dma(eng, dv, sv, after=after)


def gemm(ctx, wT, Ktot, M, act_at, N, evac, *, stack, nt=512, mt=128,
         w_eng="gpsimd", pipe=True, act_tok="auto", pre=False):
    k = ctx.k
    kgs = [(g * 128, min(128, Ktot - g * 128)) for g in range((Ktot + 127) // 128)]
    if pre:
        assert mt == 128 and Ktot % 128 == 0 and M % 128 == 0
    WS = contextlib.ExitStack()
    wbuf = [ctx.sb("gw", [128, len(kgs) * mt], BF16, stack=WS) for _ in range(2)]
    wtok = [None, None]
    consumed = [None, None]
    if not hasattr(ctx, "bank_read"):
        ctx.bank_read = {}
    if act_tok == "auto":
        act_tok = getattr(ctx, "act_tok", None)
    start_tok = k.last
    prev_evac_end = None
    for mi, m0 in enumerate(range(0, M, mt)):
        mw = min(mt, M - m0)
        buf = mi % 2
        wsb = wbuf[buf]
        nfull = sum(1 for (k0, kw) in kgs if kw == 128)
        wdeps = [t for t in (consumed[buf], start_tok if mi == 0 else None)
                 if t is not None]
        if pre:
            wtok[buf] = k.dma(w_eng, wsb[:, 0:len(kgs) * mt],
                              wT[m0:m0 + 128, :],
                              after=(wdeps if pipe else "chain"))
        else:
            if nfull:
                wtok[buf] = dma_w(ctx, wsb, wT, nfull, mt, m0, mw, eng=w_eng,
                                  after=(wdeps if pipe else "chain"))
            for g, (k0, kw) in enumerate(kgs):
                if kw != 128:
                    wtok[buf] = k.dma(w_eng, wsb[0:kw, g * mt:g * mt + mw],
                                      wT[k0:k0 + kw, m0:m0 + mw],
                                      after=("chain" if (pipe and nfull) else
                                             (wdeps if pipe else "chain")))
        last_mm = None if mi == 0 else consumed[1 - buf]
        for n0 in range(0, N, nt):
            nw = min(nt, N - n0)
            b = ctx.bank()
            for g, (k0, kw) in enumerate(kgs):
                deps = ([last_mm] if pipe else "chain")
                if pipe and g == 0:
                    deps = [t for t in (wtok[buf], ctx.bank_read.get(id(b)),
                                        last_mm,
                                        act_tok if (mi == 0 and n0 == 0) else None)
                            if t is not None]
                last_mm = k.op("tensor", (lambda b=b, g=g, kw=kw, mw=mw, nw=nw,
                                          n0=n0, wsb=wsb, s=(g == 0),
                                          e=(g == len(kgs) - 1):
                                          (lambda t: t.matmul(
                                              b[0:mw, 0:nw],
                                              wsb[0:kw, g * mt:g * mt + mw],
                                              act_at(g, n0, nw)[0:kw, :],
                                              start=s, stop=e)))(),
                               after=deps)
            if pipe:
                k.last = last_mm
                k.extra_next_deps = [prev_evac_end]
            evac(b, m0, mw, n0, nw)
            ctx.bank_read[id(b)] = k.last
            prev_evac_end = k.last
        consumed[buf] = last_mm
    WS.close()


def gemm2(ctx, wTa, wTb, Ktot, M, act_at, N, evac, *, stack, nt=512,
          mt=128, w_eng="gpsimd", act_tok="auto", pre=False):
    k = ctx.k
    kgs = [(g * 128, min(128, Ktot - g * 128)) for g in range((Ktot + 127) // 128)]
    if pre:
        assert mt == 128 and Ktot % 128 == 0 and M % 128 == 0
    WS = contextlib.ExitStack()
    wbufa = [ctx.sb("g2a", [128, len(kgs) * mt], BF16, stack=WS) for _ in range(2)]
    wbufb = [ctx.sb("g2b", [128, len(kgs) * mt], BF16, stack=WS) for _ in range(2)]
    wtok = [None, None]
    consumed = [None, None]
    if not hasattr(ctx, "bank_read"):
        ctx.bank_read = {}
    if act_tok == "auto":
        act_tok = getattr(ctx, "act_tok", None)
    start_tok = k.last
    prev_evac_end = None
    for mi, m0 in enumerate(range(0, M, mt)):
        mw = min(mt, M - m0)
        buf = mi % 2
        wdeps = [t for t in (consumed[buf], start_tok if mi == 0 else None)
                 if t is not None]
        if pre:
            k.dma(w_eng, wbufa[buf][:, 0:len(kgs) * mt], wTa[m0:m0 + 128, :],
                  after=wdeps or None)
            wtok[buf] = k.dma(w_eng, wbufb[buf][:, 0:len(kgs) * mt],
                              wTb[m0:m0 + 128, :], after="chain")
        else:
            dma_w(ctx, wbufa[buf], wTa, len(kgs), mt, m0, mw, eng=w_eng,
                  after=wdeps or None)
            wtok[buf] = dma_w(ctx, wbufb[buf], wTb, len(kgs), mt, m0, mw,
                              eng=w_eng, after="chain")
        last_mm = None if mi == 0 else consumed[1 - buf]
        for n0 in range(0, N, nt):
            nw = min(nt, N - n0)
            bA = ctx.bank()
            bB = ctx.bank()
            for (bb, wb) in ((bA, wbufa[buf]), (bB, wbufb[buf])):
                for g, (k0, kw) in enumerate(kgs):
                    deps = [last_mm]
                    if g == 0 and bb is bA:
                        deps = [t for t in (
                            wtok[buf], ctx.bank_read.get(id(bA)),
                            ctx.bank_read.get(id(bB)), last_mm,
                            act_tok if (mi == 0 and n0 == 0) else None)
                            if t is not None]
                    last_mm = k.op("tensor", (lambda bb=bb, wb=wb, g=g, kw=kw,
                                              mw=mw, nw=nw, n0=n0, s=(g == 0),
                                              e=(g == len(kgs) - 1):
                                              (lambda t: t.matmul(
                                                  bb[0:mw, 0:nw],
                                                  wb[0:kw, g * mt:g * mt + mw],
                                                  act_at(g, n0, nw)[0:kw, :],
                                                  start=s, stop=e)))(),
                                   after=deps)
            k.last = last_mm
            k.extra_next_deps = [prev_evac_end]
            evac(bA, bB, m0, mw, n0, nw)
            ctx.bank_read[id(bA)] = k.last
            ctx.bank_read[id(bB)] = k.last
            prev_evac_end = k.last
        consumed[buf] = last_mm
    WS.close()


def act_loader(ctx, src_dram, Ktot, N, *, stack, dt=BF16):
    k = ctx.k
    kg = (Ktot + 127) // 128
    sbt = ctx.sb("act", [128, kg * N], dt, stack=stack)
    tok = None
    for g in range(kg):
        kw = min(128, Ktot - g * 128)
        tok = k.dma("sync", sbt[0:kw, g * N:(g + 1) * N],
                    src_dram[g * 128:g * 128 + kw, 0:N])
    ctx.act_tok = tok

    def at(g, n0, nw):
        return sbt[0:128, g * N + n0:g * N + n0 + nw]
    return sbt, at



def ln_pass(ctx, x_dram, y_dram, stack, *, N=NT, Ktot=D, eps=1e-5,
            make_bc=True, write_y=True, src_bf=False, gw=None, gb=None):
    k = ctx.k
    S = stack
    kg = Ktot // 128
    mu = ctx.sb("mu", [1, N], stack=S)
    r = ctx.sb("r", [1, N], stack=S)
    ones = ctx.ones_col_bf if src_bf else ctx.ones_col
    xdt = BF16 if src_bf else F32
    with contextlib.ExitStack() as S2:
        xt = [ctx.sb("lnx", [128, 512], xdt, stack=S2) for _ in range(2)]
        sq = ctx.sb("lnsq", [128, 512], stack=S2)
        row_a = ctx.sb("lnra", [1, 512], stack=S2)
        row_b = ctx.sb("lnrb", [1, 512], stack=S2)
        for n0 in range(0, N, 512):
            bs = ctx.bank()
            bq = ctx.bank()
            for g in range(kg):
                x_sb = xt[g % 2]
                k.dma("sync", x_sb[:, :], x_dram[g * 128:(g + 1) * 128, n0:n0 + 512])
                k.op("tensor", (lambda bs=bs, x_sb=x_sb, s=(g == 0), e=(g == kg - 1):
                                (lambda t: t.matmul(bs[0:1, :], ones[:, :],
                                                    x_sb[:, :], start=s, stop=e)))())
                k.op("vector", (lambda x_sb=x_sb: (lambda v: v.tensor_mul(
                    sq[:, :], x_sb[:, :], x_sb[:, :])))())
                k.op("tensor", (lambda bq=bq, s=(g == 0), e=(g == kg - 1):
                                (lambda t: t.matmul(bq[0:1, :], ctx.ones_col[:, :],
                                                    sq[:, :], start=s, stop=e)))())
            k.op("vector", (lambda bs=bs, n0=n0: (lambda v: v.tensor_scalar(
                mu[0:1, n0:n0 + 512], bs[0:1, :], 1.0 / Ktot, None, ALU.mult)))())
            k.op("vector", (lambda bq=bq: (lambda v: v.tensor_scalar(
                row_a[0:1, :], bq[0:1, :], 1.0 / Ktot, None, ALU.mult)))())
            k.op("vector", (lambda n0=n0: (lambda v: v.tensor_mul(
                row_b[0:1, :], mu[0:1, n0:n0 + 512], mu[0:1, n0:n0 + 512])))())
            k.op("vector", lambda v: v.tensor_sub(row_a[0:1, :], row_a[0:1, :],
                                                  row_b[0:1, :]))
            k.op("vector", (lambda e=float(eps): (lambda v: v.tensor_scalar(
                row_a[0:1, :], row_a[0:1, :], e, None, ALU.add)))())
            k.op("scalar", lambda s: s.activation(row_b[0:1, :], row_a[0:1, :],
                                                  AF.Sqrt))
            k.op("vector", (lambda n0=n0: (lambda v: v.reciprocal(
                r[0:1, n0:n0 + 512], row_b[0:1, :])))())
    rbc = cbc = None
    if make_bc:
        rbc = ctx.sb("rbc", [128, N], stack=S)
        cbc = ctx.sb("cbc", [128, N], stack=S)
        with contextlib.ExitStack() as S2:
            nmr = ctx.sb("nmr", [1, N], stack=S2)
            k.op("vector", lambda v: v.tensor_mul(nmr[0:1, :], mu[0:1, :], r[0:1, :]))
            k.op("vector", lambda v: v.tensor_scalar(nmr[0:1, :], nmr[0:1, :],
                                                     -1.0, None, ALU.mult))
            bcast_row(ctx, r, rbc, N)
            bcast_row(ctx, nmr, cbc, N)
            if write_y:
                yt = ctx.sb("lny", [128, 512], BF16, stack=S2)
                tmp = ctx.sb("lnt", [128, 512], stack=S2)
                xs = ctx.sb("lnx3", [128, 512], xdt, stack=S2)
                for g in range(kg):
                    for n0 in range(0, N, 512):
                        k.dma("sync", xs[:, :],
                              x_dram[g * 128:(g + 1) * 128, n0:n0 + 512])
                        k.op("vector", (lambda n0=n0: (lambda v: v.tensor_mul(
                            tmp[:, :], xs[:, :], rbc[:, n0:n0 + 512])))())
                        if gw is None:
                            k.op("vector", (lambda n0=n0: (lambda v: v.tensor_add(
                                yt[:, :], tmp[:, :], cbc[:, n0:n0 + 512])))())
                        else:
                            k.op("vector", (lambda n0=n0: (lambda v: v.tensor_add(
                                tmp[:, :], tmp[:, :], cbc[:, n0:n0 + 512])))())
                            k.op("vector", (lambda g=g: (lambda v: v.tensor_scalar(
                                yt[:, :], tmp[:, :], gw[:, g:g + 1], gb[:, g:g + 1],
                                ALU.mult, ALU.add)))())
                        k.dma("sync", y_dram[g * 128:(g + 1) * 128, n0:n0 + 512],
                              yt[:, :])
    ctx.fence()
    return mu, r, rbc, cbc



def l0_attn(ctx, x_dram, wq, wk, wv, wout, x_out, scratch):
    k = ctx.k
    H = 16
    y_dram = scratch["by0"]
    q_dram = scratch["by1"]
    kk_dram = scratch["by2"]
    v_dram = scratch["bt0"]
    z_dram = scratch["by3"]

    with contextlib.ExitStack() as S0:
        mu, r, rbc, cbc = ln_pass(ctx, x_dram, y_dram, S0)

        with contextlib.ExitStack() as S1:
            act_sb, act_at = act_loader(ctx, y_dram, D, NT, stack=S1)
            ev = ctx.sb("ev", [128, 512], BF16, stack=S1)

            def mk_evac(dst):
                def evac(b, m0, mw, n0, nw):
                    k.op("vector", (lambda b=b, mw=mw, nw=nw: (lambda v: v.tensor_copy(
                        ev[0:mw, 0:nw], b[0:mw, 0:nw])))())
                    k.dma("sync", dst[m0:m0 + mw, n0:n0 + nw], ev[0:mw, 0:nw])
                return evac
            gemm(ctx, wq, D, D, act_at, NT, mk_evac(q_dram), stack=S1, pre=True)
            gemm(ctx, wk, D, D, act_at, NT, mk_evac(kk_dram), stack=S1, pre=True)
            wv_sb = ctx.sb("wv", [128, 8 * 1024], BF16, stack=S1)
            for g in range(8):
                k.dma("gpsimd", wv_sb[:, g * 1024:(g + 1) * 1024],
                      wv[g * 128:(g + 1) * 128, :])
            for t0 in range(0, NT, 128):
                for f0 in range(0, D, 512):
                    b = ctx.bank()
                    for g in range(8):
                        k.op("tensor", (lambda b=b, g=g, t0=t0, f0=f0,
                                        s=(g == 0), e=(g == 7):
                                        (lambda t: t.matmul(
                                            b[:, :],
                                            act_sb[:, g * NT + t0:g * NT + t0 + 128],
                                            wv_sb[:, g * 1024 + f0:g * 1024 + f0 + 512],
                                            start=s, stop=e)))())
                    k.op("vector", (lambda b=b: (lambda v: v.tensor_copy(
                        ev[:, :], b[:, :])))())
                    k.dma("sync", v_dram[t0:t0 + 128, f0:f0 + 512], ev[:, :])

        flash_heads(ctx, q_dram, kk_dram, v_dram, z_dram, S0, H=16, DH=64,
                    scale=0.125, causal=False)

        with contextlib.ExitStack() as S3:
            act_sb, act_at = act_loader(ctx, z_dram, D, NT, stack=S3)
            xs = ctx.sb("xs", [128, 512], stack=S3)
            t1 = ctx.sb("t1", [128, 512], stack=S3)

            def evac(b, m0, mw, n0, nw):
                k.dma("sync", xs[0:mw, 0:nw], x_dram[m0:m0 + mw, n0:n0 + nw])
                k.op("vector", (lambda b=b, m0=m0, mw=mw, n0=n0, nw=nw:
                                (lambda v: v.tensor_mul(
                                    t1[0:mw, 0:nw], xs[0:mw, 0:nw],
                                    rbc[0:mw, n0:n0 + nw])))())
                k.op("vector", (lambda b=b, mw=mw, n0=n0, nw=nw:
                                (lambda v: v.tensor_add(
                                    t1[0:mw, 0:nw], t1[0:mw, 0:nw],
                                    b[0:mw, 0:nw])))())
                k.op("vector", (lambda mw=mw, n0=n0, nw=nw: (lambda v: v.tensor_add(
                    t1[0:mw, 0:nw], t1[0:mw, 0:nw], cbc[0:mw, n0:n0 + nw])))())
                k.op("vector", (lambda mw=mw, nw=nw: (lambda v: v.tensor_add(
                    t1[0:mw, 0:nw], t1[0:mw, 0:nw], xs[0:mw, 0:nw])))())
                k.dma("sync", x_out[m0:m0 + mw, n0:n0 + nw], t1[0:mw, 0:nw])
            gemm(ctx, wout, D, D, act_at, NT, evac, stack=S3, pre=True)



def make_scratch(nc):
    s = {}
    for i in range(5):
        s[f"by{i}"] = nc.dram_tensor(f"scr_by{i}", [D, NT], BF16).ap()
    s["bt0"] = nc.dram_tensor("scr_bt0", [NT, D], BF16).ap()
    s["f8"] = nc.dram_tensor("scr_f8", [96, NT], F32).ap()
    for i in range(5):
        s[f"bys{i}"] = nc.dram_tensor(f"scr_bys{i}", [D, NT], BF16).ap()
    s["f16"] = nc.dram_tensor("scr_f16", [16, 16], F32).ap()
    s["fx0"] = nc.dram_tensor("scr_fx0", [D, NT], F32).ap()
    s["bh0"] = nc.dram_tensor("scr_bh0", [2 * D, NT], BF16).ap()
    s["l3y"] = nc.dram_tensor("scr_l3y", [8 * D, 512], BF16).ap()
    s["l3q"] = nc.dram_tensor("scr_l3q", [8 * D, 512], BF16).ap()
    s["l3k"] = nc.dram_tensor("scr_l3k", [8 * D, 512], BF16).ap()
    s["l3v"] = nc.dram_tensor("scr_l3v", [8 * D, 512], BF16).ap()
    return s



def topk2_combine(ctx, logits_sb, combine_dst, E, N, stack, to_dram=False):
    k = ctx.k
    BIG = 1.0e30
    with contextlib.ExitStack() as S:
        tm = ctx.sb("tk_tm", [128, E], stack=S)
        m1 = ctx.sb("tk_m1", [128, 1], stack=S)
        m2 = ctx.sb("tk_m2", [128, 1], stack=S)
        w1 = ctx.sb("tk_w1", [128, 1], stack=S)
        w2 = ctx.sb("tk_w2", [128, 1], stack=S)
        eq1 = ctx.sb("tk_e1", [128, E], stack=S)
        eq2 = ctx.sb("tk_e2", [128, E], stack=S)
        l2 = ctx.sb("tk_l2", [128, E], stack=S)
        cm = ctx.sb("tk_cm", [128, E], stack=S)
        cm2 = ctx.sb("tk_cm2", [E, 128], stack=S)
        d12 = ctx.sb("tk_d", [128, 1], stack=S)
        for t0 in range(0, N, 128):
            bp = ctx.ps_aux
            k.op("tensor", (lambda t0=t0: (lambda t: t.transpose(
                bp[0:128, 0:E], logits_sb[0:E, t0:t0 + 128],
                ctx.ident_f[0:E, 0:E])))())
            k.op("vector", lambda v: v.tensor_copy(tm[:, :], bp[0:128, 0:E]))
            k.op("vector", lambda v: v.tensor_reduce(m1[:, :], tm[:, :], AX.X, ALU.max))
            k.op("vector", lambda v: v.tensor_scalar(eq1[:, :], tm[:, :],
                                                     m1[:, :], None, ALU.is_equal))
            k.op("vector", lambda v: v.tensor_scalar(l2[:, :], eq1[:, :], -BIG,
                                                     None, ALU.mult))
            k.op("vector", lambda v: v.tensor_add(l2[:, :], l2[:, :], tm[:, :]))
            k.op("vector", lambda v: v.tensor_reduce(m2[:, :], l2[:, :], AX.X, ALU.max))
            k.op("vector", lambda v: v.tensor_scalar(eq2[:, :], l2[:, :],
                                                     m2[:, :], None, ALU.is_equal))
            k.op("vector", lambda v: v.tensor_sub(d12[:, :], m1[:, :], m2[:, :]))
            k.op("scalar", lambda s: s.activation(w1[:, :], d12[:, :], AF.Sigmoid))
            k.op("scalar", lambda s: s.activation(w2[:, :], d12[:, :], AF.Sigmoid,
                                                  scale=-1.0))
            k.op("vector", lambda v: v.tensor_scalar(cm[:, :], eq1[:, :],
                                                     w1[:, :], None, ALU.mult))
            k.op("vector", lambda v: v.tensor_scalar(eq2[:, :], eq2[:, :],
                                                     w2[:, :], None, ALU.mult))
            k.op("vector", lambda v: v.tensor_add(cm[:, :], cm[:, :], eq2[:, :]))
            k.op("tensor", (lambda t0=t0: (lambda t: t.transpose(
                bp[0:E, 0:128], cm[0:128, 0:E], ctx.ident_f[:, :])))())
            if to_dram:
                k.op("vector", lambda v: v.tensor_copy(cm2[0:E, 0:128],
                                                       bp[0:E, 0:128]))
                k.dma("sync", combine_dst[0:E, t0:t0 + 128], cm2[0:E, 0:128])
            else:
                k.op("vector", (lambda t0=t0: (lambda v: v.tensor_copy(
                    combine_dst[0:E, t0:t0 + 128], bp[0:E, 0:128])))())
    ctx.fence()


def fp32_logits(ctx, x_dram, rbc, cbc, wT_sb_at, E, logits_sb, bias_col, stack,
                N=NT, kg=8, gw=None, gb=None):
    k = ctx.k
    with contextlib.ExitStack() as S:
        xs = ctx.sb("lg_x", [128, 512], stack=S)
        yf = ctx.sb("lg_y", [128, 512], stack=S)
        for n0 in range(0, N, 512):
            b = ctx.bank()
            for g in range(kg):
                k.dma("sync", xs[:, :], x_dram[g * 128:(g + 1) * 128, n0:n0 + 512])
                k.op("vector", (lambda n0=n0: (lambda v: v.tensor_mul(
                    yf[:, :], xs[:, :], rbc[:, n0:n0 + 512])))())
                k.op("vector", (lambda n0=n0: (lambda v: v.tensor_add(
                    yf[:, :], yf[:, :], cbc[:, n0:n0 + 512])))())
                if gw is not None:
                    k.op("vector", (lambda g=g: (lambda v: v.tensor_scalar(
                        yf[:, :], yf[:, :], gw[:, g:g + 1], gb[:, g:g + 1],
                        ALU.mult, ALU.add)))())
                k.op("tensor", (lambda b=b, g=g, s=(g == 0), e=(g == kg - 1):
                                (lambda t: t.matmul(b[0:E, 0:512], wT_sb_at(g),
                                                    yf[:, :], start=s, stop=e)))())
            if bias_col is not None:
                k.op("vector", (lambda b=b, n0=n0: (lambda v: v.tensor_scalar(
                    logits_sb[0:E, n0:n0 + 512], b[0:E, 0:512], bias_col[0:E, :],
                    None, ALU.add)))())
            else:
                k.op("vector", (lambda b=b, n0=n0: (lambda v: v.tensor_copy(
                    logits_sb[0:E, n0:n0 + 512], b[0:E, 0:512])))())
    ctx.fence()


def l0_ffn(ctx, x_dram, router_wT, expert_bias, w1T, w3T, w2T, sw1T, sw3T,
           sw2T, x_out, scratch):
    k = ctx.k
    E, HID = 8, 2048
    y_dram = scratch["by0"]

    combine_dram = scratch["f8"]
    with contextlib.ExitStack() as SC:
        with contextlib.ExitStack() as S0:
            mu, r, rbc, cbc = ln_pass(ctx, x_dram, y_dram, S0)
            logits_sb = ctx.sb("lgt", [E, NT], stack=S0)
            rw_sb = ctx.sb("rw", [128, 8 * E], stack=S0)
            bias_col = ctx.sb("bcol", [E, 1], stack=S0)
            for g in range(8):
                k.dma("sync", rw_sb[:, g * E:(g + 1) * E],
                      router_wT[g * 128:(g + 1) * 128, :])
            k.dma("sync", bias_col[:, :], expert_bias)
            fp32_logits(ctx, x_dram, rbc, cbc,
                        lambda g: rw_sb[:, g * E:(g + 1) * E], E, logits_sb,
                        bias_col, S0)
            topk2_combine(ctx, logits_sb, combine_dram, E, NT, S0, to_dram=True)

        with contextlib.ExitStack() as S2:
            NH = 2048
            act_sb = ctx.sb("ffn_y", [128, 8 * NH], BF16, stack=S2)
            g_buf = ctx.sb("ffn_g", [128, 16 * NH], BF16, stack=S2)
            acc = ctx.sb("ffn_acc", [128, 8 * NH], F32, stack=S2)
            cw_sb = ctx.sb("ffn_cw", [128, NH], F32, stack=S2)
            crow = ctx.sb("ffn_cr", [1, NH], F32, stack=S2)
            sil = ctx.sb("ffn_sil", [128, 512], F32, stack=S2)
            t2 = ctx.sb("ffn_t2", [128, 512], F32, stack=S2)
            xs = ctx.sb("ffn_x", [128, 512], F32, stack=S2)

            exps = [(sw1T[i], sw3T[i], sw2T[i], None) for i in range(sw1T.shape[0])]
            exps += [(w1T[e], w3T[e], w2T[e], e) for e in range(E)]

            for nh in range(2):
                nbase = nh * NH
                atok = None
                for g in range(8):
                    atok = k.dma("sync", act_sb[:, g * NH:(g + 1) * NH],
                                 y_dram[g * 128:(g + 1) * 128, nbase:nbase + NH])
                at1 = lambda g, n0, nw: act_sb[:, g * NH + n0:g * NH + n0 + nw]
                at2 = lambda g, n0, nw: g_buf[:, g * NH + n0:g * NH + n0 + nw]
                for ei, (we1, we3, we2, ce) in enumerate(exps):
                    if ce is not None:
                        k.dma("sync", crow[0:1, :],
                              combine_dram[ce:ce + 1, nbase:nbase + NH])
                        bcast_row(ctx, crow[0:1, :], cw_sb, NH)
                    cwtok = k.last

                    def evac_g(bA, bB, m0, mw, n0, nw):
                        k.op("scalar", (lambda bA=bA, nw=nw: (lambda s: s.activation(
                            sil[:, 0:nw], bA[0:mw, 0:nw], AF.Silu)))())
                        k.op("vector", (lambda bB=bB, m0=m0, n0=n0, nw=nw:
                                        (lambda v: v.tensor_mul(
                                            g_buf[:, m0 * NH // 128 + n0:
                                                  m0 * NH // 128 + n0 + nw],
                                            sil[:, 0:nw], bB[0:mw, 0:nw])))())
                    gemm2(ctx, we1, we3, D, HID, at1, NH, evac_g, stack=S2,
                          act_tok=atok if ei == 0 else None, pre=True)
                    s1tok = k.last

                    def evac_o(b, m0, mw, n0, nw, ce=ce, ei=ei):
                        dst = acc[:, m0 * NH // 128 + n0:m0 * NH // 128 + n0 + nw]
                        if ce is None:
                            if ei == 0:
                                k.op("vector", (lambda b=b, dst=dst, mw=mw, nw=nw:
                                                (lambda v: v.tensor_copy(
                                                    dst, b[0:mw, 0:nw])))())
                            else:
                                k.op("vector", (lambda b=b, dst=dst, mw=mw, nw=nw:
                                                (lambda v: v.tensor_add(
                                                    dst, dst, b[0:mw, 0:nw])))())
                        else:
                            k.op("vector", (lambda b=b, n0=n0, mw=mw, nw=nw:
                                            (lambda v: v.tensor_mul(
                                                t2[:, 0:nw], b[0:mw, 0:nw],
                                                cw_sb[:, n0:n0 + nw])))())
                            k.op("vector", (lambda dst=dst, nw=nw:
                                            (lambda v: v.tensor_add(
                                                dst, dst, t2[:, 0:nw])))())
                    gemm(ctx, we2, HID, D, at2, NH, evac_o, stack=S2,
                         act_tok=s1tok, pre=True)
                for g in range(8):
                    for ni in range(4):
                        n0 = ni * 512
                        k.dma("sync", xs[:, :],
                              x_dram[g * 128:(g + 1) * 128, nbase + n0:nbase + n0 + 512])
                        k.op("vector", (lambda g=g, n0=n0: (lambda v: v.tensor_add(
                            t2[:, :], xs[:, :], acc[:, g * NH + n0:g * NH + n0 + 512])))())
                        k.dma("sync", x_out[g * 128:(g + 1) * 128,
                                            nbase + n0:nbase + n0 + 512], t2[:, :])



def l1_attn(ctx, x_dram, w_writeT, w_readT, w_betaT, decay_col, x_out, scratch,
            dbg=None):
    def dump(row, ap, P, W):
        if dbg is None:
            return
        t = ctx.dbgt
        ctx.k.op("vector", lambda v: v.tensor_copy(t[0:P, 0:W], ap))
        ctx.k.dma("sync", dbg[row:row + P, 0:W], t[0:P, 0:W],
                  allow_slow_non_contiguous=True)
    k = ctx.k
    H, d, C = 8, 128, 64
    NCH = NT // C
    y_dram = scratch["by0"]
    v_dram = scratch["by1"]
    o_dram = scratch["by2"]
    f8 = scratch["f8"]
    f1 = scratch["f16"]

    with contextlib.ExitStack() as S0:
        mu, r, rbc, cbc = ln_pass(ctx, x_dram, y_dram, S0)

        with contextlib.ExitStack() as S1:
            act_sb, act_at = act_loader(ctx, y_dram, D, NT, stack=S1)
            ev = ctx.sb("ev", [128, 512], BF16, stack=S1)

            def evac_v(b, m0, mw, n0, nw):
                k.op("vector", (lambda b=b, mw=mw, nw=nw: (lambda v: v.tensor_copy(
                    ev[0:mw, 0:nw], b[0:mw, 0:nw])))())
                k.dma("sync", v_dram[m0:m0 + mw, n0:n0 + nw], ev[0:mw, 0:nw])
            gemm(ctx, w_writeT, D, D, act_at, NT, evac_v, stack=S1, pre=True)

            bsb = ctx.sb("bsb", [H, 512], stack=S1)

            def evac_b(b, m0, mw, n0, nw):
                k.op("scalar", (lambda b=b, mw=mw, nw=nw: (lambda s: s.activation(
                    bsb[0:mw, 0:nw], b[0:mw, 0:nw], AF.Sigmoid)))())
                k.dma("sync", f8[m0:m0 + mw, n0:n0 + nw], bsb[0:mw, 0:nw])
            gemm(ctx, w_betaT, D, H, act_at, NT, evac_b, stack=S1)

            dcol = ctx.sb("dcol", [H, 1], stack=S1)
            k.dma("sync", dcol[:, :], decay_col)
            k.op("scalar", lambda s: s.activation(dcol[:, :], dcol[:, :], AF.Sigmoid))
            k.op("scalar", lambda s: s.activation(dcol[:, :], dcol[:, :], AF.Ln))
            k.dma("sync", f1[0:H, 0:1], dcol[:, :],
                  allow_slow_non_contiguous=True)

        C2 = 2 * C
        io64 = ctx.sb("io64", [C, C], stack=S0)
        cu_one = ctx.sb("cu_one", [C, C], stack=S0)
        io128 = ctx.sb("io128", [C2, C2], stack=S0)
        sl128 = ctx.sb("sl128", [C2, C2], stack=S0)
        su128 = ctx.sb("su128", [C2, C2], stack=S0)
        bd2 = ctx.sb("bd2", [C2, C2], stack=S0)
        mcorn = ctx.sb("mcorn", [C2, C2], stack=S0)
        row1 = ctx.sb("row1", [1, C], stack=S0)
        row2 = ctx.sb("row2", [1, C], stack=S0)
        k.op("gpsimd", lambda g: g.iota(io64[:, :], pattern=[[-1, C]], base=0,
                                        channel_multiplier=1,
                                        allow_small_or_imprecise_dtypes=True))
        k.op("gpsimd", lambda g: g.iota(io128[:, :], pattern=[[-1, C2]], base=0,
                                        channel_multiplier=1,
                                        allow_small_or_imprecise_dtypes=True))
        k.op("gpsimd", lambda g: g.memset(cu_one[:, :], 0.0))
        k.op("gpsimd", lambda g: g.affine_select(
            out=cu_one[:, :], in_=cu_one[:, :], compare_op=ALU.is_gt, fill=1.0,
            base=0, pattern=[[-1, C]], channel_multiplier=1))
        for (t, cmpo, fill, cm, pat) in (
                (sl128, ALU.is_ge, -1.0, -1, [[1, C2]]),
                (su128, ALU.is_ge, -1.0, 1, [[-1, C2]])):
            k.op("gpsimd", (lambda t=t: (lambda g: g.memset(t[:, :], 0.0)))())
            k.op("gpsimd", (lambda t=t, cmpo=cmpo, fill=fill, cm=cm, pat=pat:
                            (lambda g: g.affine_select(
                                out=t[:, :], in_=t[:, :], compare_op=cmpo,
                                fill=fill, base=0, pattern=pat,
                                channel_multiplier=cm)))())
        k.op("gpsimd", lambda g: g.memset(mcorn[:, :], 1.0))
        k.op("gpsimd", lambda g: g.affine_select(
            out=mcorn[:, :], in_=mcorn[:, :], compare_op=ALU.is_ge, fill=0.0,
            base=C - 1, pattern=[[0, C2]], channel_multiplier=-1))
        k.op("gpsimd", lambda g: g.affine_select(
            out=mcorn[:, :], in_=mcorn[:, :], compare_op=ALU.is_ge, fill=0.0,
            base=C - 1, pattern=[[-1, C2]], channel_multiplier=0))
        k.op("gpsimd", lambda g: g.memset(bd2[:, :], 1.0))
        k.op("gpsimd", lambda g: g.affine_select(
            out=bd2[:, :], in_=bd2[:, :], compare_op=ALU.is_ge, fill=0.0,
            base=-C, pattern=[[0, C2]], channel_multiplier=1))
        k.op("gpsimd", lambda g: g.affine_select(
            out=bd2[:, :], in_=bd2[:, :], compare_op=ALU.is_ge, fill=0.0,
            base=-C, pattern=[[1, C2]], channel_multiplier=0))
        k.op("vector", lambda v: v.tensor_add(bd2[:, :], bd2[:, :], mcorn[:, :]))
        k.op("vector", lambda v: v.tensor_mul(sl128[:, :], sl128[:, :], bd2[:, :]))
        k.op("vector", lambda v: v.tensor_mul(su128[:, :], su128[:, :], bd2[:, :]))
        k.op("gpsimd", lambda g: g.iota(row1[:, :], pattern=[[1, C]], base=1,
                                        channel_multiplier=0,
                                        allow_small_or_imprecise_dtypes=True))
        k.op("gpsimd", lambda g: g.iota(row2[:, :], pattern=[[-1, C]], base=C - 1,
                                        channel_multiplier=0,
                                        allow_small_or_imprecise_dtypes=True))

        with contextlib.ExitStack() as S2:
            xh = ctx.sb("xh", [128, NT], BF16, stack=S2)
            rk = ctx.sb("rk", [128, NT], BF16, stack=S2)
            wk = ctx.sb("wk", [128, NT], BF16, stack=S2)
            wkb = ctx.sb("wkb", [128, NT], BF16, stack=S2)
            vb = ctx.sb("vb", [128, NT], BF16, stack=S2)
            o_sb = ctx.sb("o_sb", [128, NT], BF16, stack=S2)
            rkde_a = ctx.sb("rkde_a", [128, NT], BF16, stack=S2)
            wkbde_a = ctx.sb("wkbde_a", [128, NT], BF16, stack=S2)
            wkdw_a = ctx.sb("wkdw_a", [128, NT], BF16, stack=S2)
            bbc = ctx.sb("bbc", [128, NT], BF16, stack=S2)
            sqr = ctx.sb("sqr", [128, 512], F32, stack=S2)
            ssr = ctx.sb("ssr", [1, NT], F32, stack=S2)
            rvb = ctx.sb("rvb", [128, 512], BF16, stack=S2)
            brow_h = ctx.sb("browh", [1, NT], F32, stack=S2)
            lg_s = ctx.sb("lg_s", [1, 1], F32, stack=S2)
            lg128 = ctx.sb("lg128", [C2, 1], F32, stack=S2)
            nlg128 = ctx.sb("nlg128", [C2, 1], F32, stack=S2)
            e_p128 = ctx.sb("e_p128", [C2, C2], F32, stack=S2)
            e_m128 = ctx.sb("e_m128", [C2, C2], F32, stack=S2)
            M1 = ctx.sb("M1", [C, C], F32, stack=S2)
            MB2 = ctx.sb("MB2", [C2, C2], F32, stack=S2)
            MBT2 = ctx.sb("MBT2", [C2, C2], F32, stack=S2)
            de_row = ctx.sb("de_row", [1, C], F32, stack=S2)
            dw_row = ctx.sb("dw_row", [1, C], F32, stack=S2)
            de_bc = ctx.sb("de_bc", [128, C], BF16, stack=S2)
            dw_bc = ctx.sb("dw_bc", [128, C], BF16, stack=S2)
            ct_bc = ctx.sb("ct_bc", [128, 1], F32, stack=S2)
            B_sb = ctx.sb("B_sb", [C2, C2], BF16, stack=S2)
            BT_sb = ctx.sb("BT_sb", [C2, C2], BF16, stack=S2)
            X_f = ctx.sb("X_f", [C2, 256], F32, stack=S2)
            X_bf = ctx.sb("X_bf", [C2, 256], BF16, stack=S2)
            wkc_dm2 = ctx.sb("wkc_dm2", [128, C2], BF16, stack=S2)
            wkdw_tA = ctx.sb("wkdw_tA", [C, 128], BF16, stack=S2)
            wkdw_tB = ctx.sb("wkdw_tB", [C, 128], BF16, stack=S2)
            attnT = ctx.sb("attnT", [C, C], BF16, stack=S2)
            vn_f = ctx.sb("vn_f", [C, 128], F32, stack=S2)
            vn_bf = ctx.sb("vn_bf", [C, 128], BF16, stack=S2)
            ob_bf = ctx.sb("ob_bf", [C, 128], BF16, stack=S2)
            S_f = ctx.sb("S_f", [128, 128], F32, stack=S2)
            S_bf = ctx.sb("S_bf", [128, 128], BF16, stack=S2)

            for h in range(H):
                k.dma("sync", xh[:, :], y_dram[h * 128:(h + 1) * 128, :])
                k.dma("sync", brow_h[0:1, :], f8[h:h + 1, :])
                for n0 in range(0, NT, 512):
                    b = ctx.bank()
                    k.op("tensor", (lambda b=b, n0=n0: (lambda t: t.matmul(
                        b[:, :], ctx.ones_row[0:1, :], brow_h[0:1, n0:n0 + 512],
                        start=True, stop=True)))())
                    k.op("vector", (lambda b=b, n0=n0: (lambda v: v.tensor_copy(
                        bbc[:, n0:n0 + 512], b[:, :])))())
                for n0 in range(0, NT, 512):
                    b = ctx.bank()
                    k.op("vector", (lambda n0=n0: (lambda v: v.tensor_mul(
                        sqr[:, :], xh[:, n0:n0 + 512], xh[:, n0:n0 + 512])))())
                    k.op("tensor", (lambda b=b: (lambda t: t.matmul(
                        b[0:1, 0:512], ctx.ones_col[:, :], sqr[:, :],
                        start=True, stop=True)))())
                    k.op("scalar", (lambda b=b, n0=n0: (lambda s: s.activation(
                        ssr[0:1, n0:n0 + 512], b[0:1, 0:512], AF.Sqrt)))())
                k.op("vector", lambda v: v.tensor_scalar(
                    ssr[0:1, :], ssr[0:1, :], 1.0e-12, None, ALU.max))
                k.op("vector", lambda v: v.reciprocal(ssr[0:1, :], ssr[0:1, :]))
                for n0 in range(0, NT, 512):
                    b = ctx.bank()
                    k.op("tensor", (lambda b=b, n0=n0: (lambda t: t.matmul(
                        b[:, :], ctx.ones_row[0:1, :], ssr[0:1, n0:n0 + 512],
                        start=True, stop=True)))())
                    k.op("vector", (lambda b=b: (lambda v: v.tensor_copy(
                        rvb[:, :], b[:, :])))())
                    k.op("vector", (lambda n0=n0: (lambda v: v.tensor_mul(
                        rk[:, n0:n0 + 512], xh[:, n0:n0 + 512], rvb[:, :])))())
                k.op("gpsimd", lambda g: g.memset(wk[:, 0:1], 0.0))
                k.op("vector", lambda v: v.tensor_copy(wk[:, 1:NT], rk[:, 0:NT - 1]))
                k.dma("sync", vb[:, :], v_dram[h * 128:(h + 1) * 128, :])
                k.op("vector", lambda v: v.tensor_mul(wkb[:, :], wk[:, :],
                                                      bbc[:, :]))
                k.op("vector", lambda v: v.tensor_mul(vb[:, :], vb[:, :],
                                                      bbc[:, :]))
                k.dma("sync", lg_s[0:1, 0:1], f1[h:h + 1, 0:1])
                bq = ctx.bank()
                k.op("tensor", (lambda bq=bq: (lambda t: t.matmul(
                    bq[0:C2, 0:1], ctx.ones_row[0:1, 0:C2],
                    lg_s[0:1, 0:1], start=True, stop=True)))())
                k.op("vector", (lambda bq=bq: (lambda v: v.tensor_copy(
                    lg128[:, :], bq[0:C2, 0:1])))())
                k.op("vector", lambda v: v.tensor_scalar(nlg128[:, :], lg128[:, :],
                                                         -1.0, None, ALU.mult))
                k.op("scalar", lambda s: s.activation(e_m128[:, :], io128[:, :],
                                                      AF.Exp, scale=nlg128[:, :]))
                k.op("scalar", lambda s: s.activation(e_p128[:, :], io128[:, :],
                                                      AF.Exp, scale=lg128[:, :]))
                k.op("vector", lambda v: v.tensor_mul(M1[:, :], e_m128[0:C, 0:C],
                                                      cu_one[:, :]))
                k.op("vector", lambda v: v.tensor_mul(MB2[:, :], e_p128[:, :],
                                                      sl128[:, :]))
                k.op("vector", lambda v: v.tensor_mul(MBT2[:, :], e_m128[:, :],
                                                      su128[:, :]))
                k.op("scalar", lambda s: s.activation(de_row[:, :], row1[:, :],
                                                      AF.Exp, scale=lg_s[:, :]))
                k.op("scalar", lambda s: s.activation(dw_row[:, :], row2[:, :],
                                                      AF.Exp, scale=lg_s[:, :]))
                bq = ctx.bank()
                k.op("tensor", (lambda bq=bq: (lambda t: t.matmul(
                    bq[0:128, 0:C], ctx.ones_row[0:1, :],
                    de_row[0:1, :], start=True, stop=True)))())
                k.op("vector", (lambda bq=bq: (lambda v: v.tensor_copy(
                    de_bc[:, :], bq[0:128, 0:C])))())
                bq = ctx.bank()
                k.op("tensor", (lambda bq=bq: (lambda t: t.matmul(
                    bq[0:128, 0:C], ctx.ones_row[0:1, :],
                    dw_row[0:1, :], start=True, stop=True)))())
                k.op("vector", (lambda bq=bq: (lambda v: v.tensor_copy(
                    dw_bc[:, :], bq[0:128, 0:C])))())
                bq = ctx.bank()
                k.op("tensor", (lambda bq=bq: (lambda t: t.matmul(
                    bq[0:128, 0:1], ctx.ones_row[0:1, :],
                    de_row[0:1, C - 1:C], start=True, stop=True)))())
                k.op("vector", (lambda bq=bq: (lambda v: v.tensor_copy(
                    ct_bc[:, :], bq[0:128, 0:1])))())
                de3 = de_bc[:, :].rearrange("p (o c) -> p o c", o=1)
                dw3 = dw_bc[:, :].rearrange("p (o c) -> p o c", o=1)
                k.op("vector", lambda v: v.tensor_mul(
                    rkde_a[:, :].rearrange("p (n c) -> p n c", c=C),
                    rk[:, :].rearrange("p (n c) -> p n c", c=C),
                    de3.broadcast_to([128, NCH, C])))
                k.op("vector", lambda v: v.tensor_mul(
                    wkbde_a[:, :].rearrange("p (n c) -> p n c", c=C),
                    wkb[:, :].rearrange("p (n c) -> p n c", c=C),
                    de3.broadcast_to([128, NCH, C])))
                k.op("vector", lambda v: v.tensor_mul(
                    wkdw_a[:, :].rearrange("p (n c) -> p n c", c=C),
                    wk[:, :].rearrange("p (n c) -> p n c", c=C),
                    dw3.broadcast_to([128, NCH, C])))
                k.op("vector", lambda v: v.memset(S_f[:, :], 0.0))
                k.op("vector", lambda v: v.memset(S_bf[:, :], 0.0))

                for pi in range(NCH // 2):
                    c0 = pi * C2
                    b1 = ctx.bank()
                    k.op("tensor", (lambda b1=b1, c0=c0: (lambda t: t.matmul(
                        b1[0:C2, 0:C2], wkb[:, c0:c0 + C2], wk[:, c0:c0 + C2],
                        start=True, stop=True)))())
                    k.op("vector", (lambda b1=b1: (lambda v: v.tensor_mul(
                        B_sb[:, :], b1[0:C2, 0:C2], MB2[:, :])))())
                    b2 = ctx.bank()
                    k.op("tensor", (lambda b2=b2, c0=c0: (lambda t: t.matmul(
                        b2[0:C2, 0:C2], wk[:, c0:c0 + C2], wkb[:, c0:c0 + C2],
                        start=True, stop=True)))())
                    k.op("vector", (lambda b2=b2: (lambda v: v.tensor_mul(
                        BT_sb[:, :], b2[0:C2, 0:C2], MBT2[:, :])))())
                    k.op("tensor", (lambda c0=c0: (lambda t: t.transpose(
                        ctx.ps_tr[0:C2, 0:128], vb[:, c0:c0 + C2],
                        ctx.ident[:, :])))())
                    k.op("vector", lambda v: v.tensor_copy(X_bf[:, 0:128],
                                                           ctx.ps_tr[0:C2, 0:128]))
                    k.op("vector", lambda v: v.tensor_copy(X_f[:, 0:128],
                                                           ctx.ps_tr[0:C2, 0:128]))
                    k.op("tensor", (lambda c0=c0: (lambda t: t.transpose(
                        ctx.ps_tr[0:C2, 128:256], wkbde_a[:, c0:c0 + C2],
                        ctx.ident[:, :])))())
                    k.op("vector", lambda v: v.tensor_copy(X_bf[:, 128:256],
                                                           ctx.ps_tr[0:C2, 128:256]))
                    k.op("vector", lambda v: v.tensor_copy(X_f[:, 128:256],
                                                           ctx.ps_tr[0:C2, 128:256]))
                    for it in range(6):
                        b3 = ctx.bank()
                        k.op("tensor", (lambda b3=b3: (lambda t: t.matmul(
                            b3[0:C2, 0:256], BT_sb[:, :], X_bf[:, :],
                            start=True, stop=True)))())
                        k.op("vector", (lambda b3=b3: (lambda v: v.tensor_add(
                            X_f[:, :], X_f[:, :], b3[0:C2, 0:256])))())
                        k.op("vector", lambda v: v.tensor_copy(X_bf[:, :], X_f[:, :]))
                        if it < 5:
                            b4 = ctx.bank()
                            b5 = ctx.bank()
                            k.op("tensor", (lambda b4=b4: (lambda t: t.matmul(
                                b4[0:C2, 0:C2], BT_sb[:, :], B_sb[:, :],
                                start=True, stop=True)))())
                            k.op("tensor", (lambda b5=b5: (lambda t: t.matmul(
                                b5[0:C2, 0:C2], B_sb[:, :], BT_sb[:, :],
                                start=True, stop=True)))())
                            k.op("vector", (lambda b4=b4: (lambda v: v.tensor_copy(
                                B_sb[:, :], b4[0:C2, 0:C2])))())
                            k.op("vector", (lambda b5=b5: (lambda v: v.tensor_copy(
                                BT_sb[:, :], b5[0:C2, 0:C2])))())
                    k.op("tensor", lambda t: t.transpose(
                        ctx.ps_tr[0:128, 0:C2], X_bf[:, 128:256],
                        ctx.ident[0:C2, 0:C2]))
                    k.op("vector", lambda v: v.tensor_copy(wkc_dm2[:, :],
                                                           ctx.ps_tr[0:128, 0:C2]))
                    k.op("tensor", (lambda c0=c0: (lambda t: t.transpose(
                        ctx.ps_tr[0:C2, 0:128], wkdw_a[:, c0:c0 + C2],
                        ctx.ident[:, :])))())
                    k.op("vector", lambda v: v.tensor_copy(wkdw_tA[:, :],
                                                           ctx.ps_tr[0:C, 0:128]))
                    k.op("vector", lambda v: v.tensor_copy(wkdw_tB[:, :],
                                                           ctx.ps_tr[C:C2, 0:128]))
                    for j in range(2):
                        cc = c0 + j * C
                        tb = j * C
                        ba = ctx.bank()
                        k.op("tensor", (lambda ba=ba, cc=cc: (lambda t: t.matmul(
                            ba[0:C, 0:C], wk[:, cc:cc + C], rk[:, cc:cc + C],
                            start=True, stop=True)))())
                        k.op("vector", (lambda ba=ba: (lambda v: v.tensor_mul(
                            attnT[:, :], ba[0:C, 0:C], M1[:, :])))())
                        bb = ctx.bank()
                        k.op("tensor", (lambda bb=bb, tb=tb: (lambda t: t.matmul(
                            bb[0:C, 0:128], wkc_dm2[:, tb:tb + C], S_bf[:, :],
                            start=True, stop=True)))())
                        k.op("vector", (lambda bb=bb, tb=tb: (lambda v: v.tensor_sub(
                            vn_f[:, :], X_f[tb:tb + C, 0:128], bb[0:C, 0:128])))())
                        k.op("vector", lambda v: v.tensor_copy(vn_bf[:, :],
                                                               vn_f[:, :]))
                        bo = ctx.bank()
                        k.op("tensor", (lambda bo=bo, cc=cc: (lambda t: t.matmul(
                            bo[0:C, 0:128], rkde_a[:, cc:cc + C], S_bf[:, :],
                            start=True, stop=False)))())
                        k.op("tensor", (lambda bo=bo: (lambda t: t.matmul(
                            bo[0:C, 0:128], attnT[:, :], vn_bf[:, :],
                            start=False, stop=True)))())
                        k.op("vector", (lambda bo=bo: (lambda v: v.tensor_copy(
                            ob_bf[:, :], bo[0:C, 0:128])))())
                        k.op("tensor", lambda t: t.transpose(
                            ctx.ps_tr[0:128, 0:C], ob_bf[:, :],
                            ctx.ident[0:C, 0:C]))
                        k.op("vector", (lambda cc=cc: (lambda v: v.tensor_copy(
                            o_sb[:, cc:cc + C], ctx.ps_tr[0:128, 0:C])))())
                        bs2 = ctx.bank()
                        wdt = wkdw_tA if j == 0 else wkdw_tB
                        k.op("tensor", (lambda bs2=bs2, wdt=wdt: (lambda t: t.matmul(
                            bs2[0:128, 0:128], wdt[:, :], vn_bf[:, :],
                            start=True, stop=True)))())
                        k.op("vector", (lambda bs2=bs2: (lambda v: v.scalar_tensor_tensor(
                            S_f[:, :], S_f[:, :], ct_bc[:, :], bs2[0:128, 0:128],
                            ALU.mult, ALU.add)))())
                        k.op("vector", lambda v: v.tensor_copy(S_bf[:, :], S_f[:, :]))
                k.dma("sync", o_dram[h * 128:(h + 1) * 128, :], o_sb[:, :])

        with contextlib.ExitStack() as S3:
            act_sb, act_at = act_loader(ctx, o_dram, D, NT, stack=S3)
            xs = ctx.sb("xs", [128, 512], stack=S3)
            t1 = ctx.sb("t1", [128, 512], stack=S3)

            def evac(b, m0, mw, n0, nw):
                k.dma("sync", xs[0:mw, 0:nw], x_dram[m0:m0 + mw, n0:n0 + nw])
                k.op("vector", (lambda mw=mw, n0=n0, nw=nw: (lambda v: v.tensor_mul(
                    t1[0:mw, 0:nw], xs[0:mw, 0:nw], rbc[0:mw, n0:n0 + nw])))())
                k.op("vector", (lambda b=b, mw=mw, nw=nw: (lambda v: v.tensor_add(
                    t1[0:mw, 0:nw], t1[0:mw, 0:nw], b[0:mw, 0:nw])))())
                k.op("vector", (lambda mw=mw, n0=n0, nw=nw: (lambda v: v.tensor_add(
                    t1[0:mw, 0:nw], t1[0:mw, 0:nw], cbc[0:mw, n0:n0 + nw])))())
                k.op("vector", (lambda mw=mw, nw=nw: (lambda v: v.tensor_add(
                    t1[0:mw, 0:nw], t1[0:mw, 0:nw], xs[0:mw, 0:nw])))())
                k.dma("sync", x_out[m0:m0 + mw, n0:n0 + nw], t1[0:mw, 0:nw])
            gemm(ctx, w_readT, D, D, act_at, NT, evac, stack=S3, pre=True)



def transpose_fmaj_to_tok(ctx, src_dram, dst_dram, stack, Ktot=D, N=NT):
    k = ctx.k
    with contextlib.ExitStack() as S:
        tin = ctx.sb("tp_i", [128, 128], BF16, stack=S)
        tout = ctx.sb("tp_o", [128, 128], BF16, stack=S)
        for f0 in range(0, Ktot, 128):
            for t0 in range(0, N, 128):
                k.dma("sync", tin[:, :], src_dram[f0:f0 + 128, t0:t0 + 128])
                k.op("tensor", lambda t: t.transpose(
                    ctx.ps_tr[0:128, 0:128], tin[:, :], ctx.ident[:, :]))
                k.op("vector", lambda v: v.tensor_copy(tout[:, :],
                                                       ctx.ps_tr[0:128, 0:128]))
                k.dma("sync", dst_dram[t0:t0 + 128, f0:f0 + 128], tout[:, :])


def flash_heads(ctx, q_dram, k_dram, vt_dram, z_dram, stack, *, H=16, DH=64,
                scale=0.125, causal=False, v_off=0):
    k = ctx.k
    with contextlib.ExitStack() as S2:
        qa = ctx.sb("qa", [DH + 1, NT], BF16, stack=S2)
        ka = ctx.sb("ka", [DH + 1, NT], BF16, stack=S2)
        sqh = ctx.sb("sqh", [DH, 512], stack=S2)
        qn = ctx.sb("qn", [1, NT], stack=S2)
        kn = ctx.sb("kn", [1, NT], stack=S2)
        mk_t = ctx.sb("mkt", [1, 1], stack=S2)
        brow = ctx.sb("brow2", [1, NT], stack=S2)
        va = ctx.sb("va", [128, 32 * (DH + 1)], BF16, stack=S2)
        pt = [ctx.sb("pt", [128, 512], BF16, stack=S2) for _ in range(2)]
        rb = ctx.sb("rb", [DH, 512], stack=S2)
        zbf = ctx.sb("zbf", [DH, 512], BF16, stack=S2)
        masks = []
        if causal:
            for j in range(4):
                m = ctx.sb(f"cmsk{j}", [128, 512], BF16, stack=S2)
                k.op("gpsimd", (lambda m=m: (lambda g: g.memset(m[:, :], 1.0)))())
                k.op("gpsimd", (lambda m=m, j=j: (lambda g: g.affine_select(
                    out=m[:, :], in_=m[:, :], compare_op=ALU.is_ge, fill=0.0,
                    base=-128 * j, pattern=[[1, 512]], channel_multiplier=-1)))())
                masks.append(m)
        for h in range(H):
            k.op("gpsimd", lambda g: g.memset(va[:, :], 1.0))
            k.dma("sync",
                  va[:, :].rearrange("p (a c) -> p a c", c=DH + 1)[:, :, 0:DH],
                  vt_dram.rearrange("(a p) f -> p a f", p=128)[
                      :, :, v_off + h * DH:v_off + (h + 1) * DH])
            k.dma("sync", qa[0:DH, :], q_dram[h * DH:(h + 1) * DH, :])
            k.dma("sync", ka[0:DH, :], k_dram[h * DH:(h + 1) * DH, :])
            k.op("gpsimd", lambda g: g.memset(ka[DH:DH + 1, :], 1.0))
            for (src, dst) in ((qa, qn), (ka, kn)):
                for n0 in range(0, NT, 512):
                    b = ctx.bank()
                    k.op("vector", (lambda src=src, n0=n0: (lambda v: v.tensor_mul(
                        sqh[:, :], src[0:DH, n0:n0 + 512],
                        src[0:DH, n0:n0 + 512])))())
                    k.op("tensor", (lambda b=b: (lambda t: t.matmul(
                        b[0:1, 0:512], ctx.ones_col[0:DH, :], sqh[:, :],
                        start=True, stop=True)))())
                    k.op("scalar", (lambda b=b, dst=dst, n0=n0:
                                    (lambda s: s.activation(
                                        dst[0:1, n0:n0 + 512], b[0:1, 0:512],
                                        AF.Sqrt)))())
            k.op("vector", lambda v: v.tensor_reduce(
                mk_t[0:1, 0:1], kn[0:1, :], AX.X, ALU.max))
            k.op("vector", (lambda sc=scale: (lambda v: v.tensor_scalar(
                brow[0:1, :], qn[0:1, :], mk_t[0:1, 0:1], sc,
                ALU.mult, ALU.mult)))())
            k.op("scalar", (lambda sc=scale: (lambda s: s.activation(
                qa[DH:DH + 1, :], brow[0:1, :], AF.Copy, scale=-1.0 / sc)))())

            prev_tail = k.last
            for n0 in range(0, NT, 512):
                bz = ctx._psum[5]
                kks = range(32) if not causal else range((n0 + 512) // 128)
                kks = list(kks)
                sc_tok = {}
                p_tok = {}
                sv_tok = {}

                def emit_score(ii, deps):
                    kki = kks[ii]
                    bs = ctx._psum[ii % 2]
                    sc_tok[ii] = k.op("tensor", (lambda bs=bs, kki=kki, n0=n0:
                                                 (lambda t: t.matmul(
                                                     bs[:, :],
                                                     ka[:, kki * 128:(kki + 1) * 128],
                                                     qa[:, n0:n0 + 512],
                                                     start=True, stop=True)))(),
                                      after=deps)
                emit_score(0, [prev_tail])
                if len(kks) > 1:
                    emit_score(1, [prev_tail])
                for ii, kki in enumerate(kks):
                    vb = va[:, kki * (DH + 1):(kki + 1) * (DH + 1)]
                    bs = ctx._psum[ii % 2]
                    pbuf = pt[ii % 2]
                    p_tok[ii] = k.op("scalar", (lambda bs=bs, pbuf=pbuf, sc=scale:
                                                (lambda s: s.activation(
                                                    pbuf[:, :], bs[:, :],
                                                    AF.Exp, scale=sc)))(),
                                     after=[t for t in (sc_tok[ii],
                                                        sv_tok.get(ii - 2))
                                            if t is not None])
                    if causal and kki * 128 >= n0:
                        mj = masks[kki - n0 // 128]
                        p_tok[ii] = k.op("vector", (lambda mj=mj, pbuf=pbuf:
                                                    (lambda v: v.tensor_mul(
                                                        pbuf[:, :], pbuf[:, :],
                                                        mj[:, :])))(),
                                         after=[p_tok[ii]])
                    if ii + 2 < len(kks):
                        emit_score(ii + 2, [p_tok[ii]])
                    sv_tok[ii] = k.op("tensor", (lambda bz=bz, vb=vb, pbuf=pbuf,
                                                 s=(ii == 0),
                                                 e=(ii == len(kks) - 1):
                                                 (lambda t: t.matmul(
                                                     bz[0:DH + 1, :], vb,
                                                     pbuf[:, :],
                                                     start=s, stop=e)))(),
                                      after=[p_tok[ii]])
                k.last = sv_tok[len(kks) - 1]
                k.op("vector", (lambda bz=bz: (lambda v: v.reciprocal(
                    brow[0:1, 0:512], bz[DH:DH + 1, 0:512])))())
                b2 = ctx._psum[2]
                k.op("tensor", lambda t: t.matmul(
                    b2[0:DH, 0:512], ctx.ones_row[0:1, 0:DH],
                    brow[0:1, 0:512], start=True, stop=True))
                k.op("vector", (lambda b2=b2: (lambda v: v.tensor_copy(
                    rb[:, :], b2[0:DH, 0:512])))())
                k.op("vector", (lambda bz=bz: (lambda v: v.tensor_mul(
                    zbf[:, :], bz[0:DH, :], rb[:, :])))())
                k.dma("sync", z_dram[h * DH:(h + 1) * DH, n0:n0 + 512], zbf[:, :])
                prev_tail = k.last
        ctx.fence()


def l1_ffn(ctx, x_dram, ln1w, ln1b, ln2w, ln2b, wqkv, bqkv, wo, bo, gate_w,
           fc1T, fc1b, fc2T, fc2b, x_out, scratch, dbg=None):
    def dump(row, ap, P, W):
        if dbg is None:
            return
        t = ctx.dbgt
        ctx.k.op("vector", lambda v: v.tensor_copy(t[0:P, 0:W], ap))
        ctx.k.dma("sync", dbg[row:row + P, 0:W], t[0:P, 0:W],
                  allow_slow_non_contiguous=True)
    """ln cols are host-layout (128, 8); bqkv (128, 24); fc1b (8,128,32);
    fc2b (8,128,8); fc1T (8,1024,4096); fc2T (8,4096,1024)."""
    k = ctx.k
    E, FD = 8, 4096
    y_dram = scratch["by0"]
    nx_dram = scratch["by1"]
    q_dram = scratch["by2"]
    kk_dram = scratch["by3"]
    vf_dram = scratch["by4"]
    vt_dram = scratch["bt0"]
    x1_dram = scratch["fx0"]
    combine_dram = scratch["f8"]

    with contextlib.ExitStack() as S0:
        mu, r, rbc, cbc = ln_pass(ctx, x_dram, y_dram, S0)

        with contextlib.ExitStack() as S1:
            l1w = ctx.sb("l1w", [128, 8], stack=S1)
            l1b = ctx.sb("l1b", [128, 8], stack=S1)
            k.dma("sync", l1w[:, :], ln1w)
            k.dma("sync", l1b[:, :], ln1b)
            ln_pass(ctx, y_dram, nx_dram, S1, src_bf=True, gw=l1w, gb=l1b)

        with contextlib.ExitStack() as S1:
            act_sb, act_at = act_loader(ctx, nx_dram, D, NT, stack=S1)
            ev = ctx.sb("ev", [128, 512], BF16, stack=S1)
            bq_sb = ctx.sb("bq_sb", [128, 24], stack=S1)
            k.dma("sync", bq_sb[:, :], bqkv)

            def evac_qkv(b, m0, mw, n0, nw):
                mi = m0 // 128
                dst = (q_dram if m0 < 1024 else kk_dram if m0 < 2048 else vf_dram)
                dd = m0 % 1024
                k.op("vector", (lambda b=b, mw=mw, nw=nw, mi=mi:
                                (lambda v: v.tensor_scalar(
                                    ev[0:mw, 0:nw], b[0:mw, 0:nw],
                                    bq_sb[:, mi:mi + 1], None, ALU.add)))())
                k.dma("sync", dst[dd:dd + mw, n0:n0 + nw], ev[0:mw, 0:nw])
            gemm(ctx, wqkv, D, 3 * D, act_at, NT, evac_qkv, stack=S1, pre=True)
        transpose_fmaj_to_tok(ctx, vf_dram, vt_dram, S0)

        flash_heads(ctx, q_dram, kk_dram, vt_dram, nx_dram, S0, H=16, DH=64,
                    scale=0.125, causal=True)

        with contextlib.ExitStack() as S3:
            act_sb, act_at = act_loader(ctx, nx_dram, D, NT, stack=S3)
            xs = ctx.sb("xs", [128, 512], stack=S3)
            t1 = ctx.sb("t1", [128, 512], stack=S3)
            bo_sb = ctx.sb("bo_sb", [128, 8], stack=S3)
            k.dma("sync", bo_sb[:, :], bo)

            def evac(b, m0, mw, n0, nw):
                k.dma("sync", xs[0:mw, 0:nw], x_dram[m0:m0 + mw, n0:n0 + nw])
                if m0 == 0 and n0 == 0:
                    dump(0, xs[0:128, 0:512], 128, 512)
                    dump(128, rbc[0:128, 0:512], 128, 512)
                    dump(256, cbc[0:128, 0:512], 128, 512)
                k.op("vector", (lambda mw=mw, n0=n0, nw=nw: (lambda v: v.tensor_mul(
                    t1[0:mw, 0:nw], xs[0:mw, 0:nw], rbc[0:mw, n0:n0 + nw])))())
                if m0 == 0 and n0 == 0:
                    dump(384, t1[0:128, 0:512], 128, 512)
                k.op("vector", (lambda b=b, mw=mw, nw=nw: (lambda v: v.tensor_add(
                    t1[0:mw, 0:nw], t1[0:mw, 0:nw], b[0:mw, 0:nw])))())
                if m0 == 0 and n0 == 0:
                    dump(512, t1[0:128, 0:512], 128, 512)
                k.op("vector", (lambda mw=mw, n0=n0, nw=nw: (lambda v: v.tensor_add(
                    t1[0:mw, 0:nw], t1[0:mw, 0:nw], cbc[0:mw, n0:n0 + nw])))())
                k.op("vector", (lambda mw=mw, nw=nw, mi=m0 // 128:
                                (lambda v: v.tensor_scalar(
                                    t1[0:mw, 0:nw], t1[0:mw, 0:nw],
                                    bo_sb[:, mi:mi + 1], None, ALU.add)))())
                k.dma("sync", x1_dram[m0:m0 + mw, n0:n0 + nw], t1[0:mw, 0:nw])
            gemm(ctx, wo, D, D, act_at, NT, evac, stack=S3, pre=True)

    nx2_dram = scratch["by2"]
    with contextlib.ExitStack() as S4:
        l2w = ctx.sb("l2w", [128, 8], stack=S4)
        l2b = ctx.sb("l2b", [128, 8], stack=S4)
        k.dma("sync", l2w[:, :], ln2w)
        k.dma("sync", l2b[:, :], ln2b)
        mu2, r2, rbc2, cbc2 = ln_pass(ctx, x1_dram, nx2_dram, S4, gw=l2w, gb=l2b)
        logits_sb = ctx.sb("lgt", [E, NT], stack=S4)
        gw_sb = ctx.sb("gwsb", [128, 8 * E], stack=S4)
        for g in range(8):
            k.dma("sync", gw_sb[:, g * E:(g + 1) * E],
                  gate_w[g * 128:(g + 1) * 128, :])
        fp32_logits(ctx, x1_dram, rbc2, cbc2,
                    lambda g: gw_sb[:, g * E:(g + 1) * E], E, logits_sb,
                    None, S4, gw=l2w, gb=l2b)
        topk2_combine(ctx, logits_sb, combine_dram, E, NT, S4, to_dram=True)

    with contextlib.ExitStack() as S5:
        NQ = 1024
        act_sb = ctx.sb("m_y", [128, 8 * NQ], BF16, stack=S5)
        g_buf = ctx.sb("m_g", [128, 32 * NQ], BF16, stack=S5)
        acc = ctx.sb("m_acc", [128, 8 * NQ], F32, stack=S5)
        cw_sb = ctx.sb("m_cw", [128, NQ], F32, stack=S5)
        crow = ctx.sb("m_cr", [1, NQ], F32, stack=S5)
        b1c = ctx.sb("m_b1", [128, 32], stack=S5)
        b2c = ctx.sb("m_b2", [128, 8], stack=S5)
        t2 = ctx.sb("m_t2", [128, 512], F32, stack=S5)
        xr5 = ctx.sb("m_x", [128, 512], F32, stack=S5)
        x1s = ctx.sb("m_x1", [128, 512], F32, stack=S5)

        at1 = lambda g, n0, nw: act_sb[:, g * NQ + n0:g * NQ + n0 + nw]
        at2 = lambda g, n0, nw: g_buf[:, g * NQ + n0:g * NQ + n0 + nw]
        for nq in range(4):
            nbase = nq * NQ
            atok = None
            for g in range(8):
                atok = k.dma("sync", act_sb[:, g * NQ:(g + 1) * NQ],
                             nx2_dram[g * 128:(g + 1) * 128, nbase:nbase + NQ])
            for e in range(E):
                k.dma("sync", crow[0:1, :], combine_dram[e:e + 1, nbase:nbase + NQ])
                bcast_row(ctx, crow[0:1, :], cw_sb, NQ)
                k.dma("sync", b1c[:, :], fc1b[e])
                k.dma("sync", b2c[:, :], fc2b[e])

                def evac_g(b, m0, mw, n0, nw):
                    k.op("scalar", (lambda b=b, m0=m0, mw=mw, n0=n0, nw=nw:
                                    (lambda s: s.activation(
                                        g_buf[:, (m0 // 128) * NQ + n0:
                                              (m0 // 128) * NQ + n0 + nw],
                                        b[0:mw, 0:nw], AF.Gelu,
                                        bias=b1c[:, m0 // 128:m0 // 128 + 1])))())
                gemm(ctx, fc1T[e], D, FD, at1, NQ, evac_g, stack=S5, pre=True,
                     act_tok=atok if e == 0 else None)
                s1tok = k.last

                def evac_o(b, m0, mw, n0, nw, e=e):
                    dst = acc[:, (m0 // 128) * NQ + n0:(m0 // 128) * NQ + n0 + nw]
                    k.op("vector", (lambda b=b, m0=m0, mw=mw, nw=nw:
                                    (lambda v: v.tensor_scalar(
                                        t2[:, 0:nw], b[0:mw, 0:nw],
                                        b2c[:, m0 // 128:m0 // 128 + 1], None,
                                        ALU.add)))())
                    k.op("vector", (lambda n0=n0, nw=nw: (lambda v: v.tensor_mul(
                        t2[:, 0:nw], t2[:, 0:nw], cw_sb[:, n0:n0 + nw])))())
                    if e == 0:
                        k.op("vector", (lambda dst=dst, nw=nw: (lambda v: v.tensor_copy(
                            dst, t2[:, 0:nw])))())
                    else:
                        k.op("vector", (lambda dst=dst, nw=nw: (lambda v: v.tensor_add(
                            dst, dst, t2[:, 0:nw])))())
                gemm(ctx, fc2T[e], FD, D, at2, NQ, evac_o, stack=S5, pre=True,
                     act_tok=s1tok)
            for g in range(8):
                for ni in range(2):
                    n0 = ni * 512
                    k.dma("sync", xr5[:, :],
                          x_dram[g * 128:(g + 1) * 128, nbase + n0:nbase + n0 + 512])
                    k.dma("sync", x1s[:, :],
                          x1_dram[g * 128:(g + 1) * 128, nbase + n0:nbase + n0 + 512])
                    k.op("vector", lambda v: v.tensor_add(t2[:, :], xr5[:, :], x1s[:, :]))
                    k.op("vector", (lambda g=g, n0=n0: (lambda v: v.tensor_add(
                        t2[:, :], t2[:, :], acc[:, g * NQ + n0:g * NQ + n0 + 512])))())
                    k.dma("sync", x_out[g * 128:(g + 1) * 128,
                                        nbase + n0:nbase + n0 + 512], t2[:, :])



def causal_dwconv(ctx, src_dram, dst_dram, wcols, taps, stack, *, act=None,
                  accum_dsts=None, N=NT):
    k = ctx.k
    PAD = 32
    with contextlib.ExitStack() as S:
        src_sb = ctx.sb("cv_src", [128, PAD + N], BF16, stack=S)
        acc = ctx.sb("cv_acc", [128, N], F32, stack=S)
        ot = ctx.sb("cv_o", [128, N], BF16, stack=S)
        jobs = accum_dsts if accum_dsts is not None else [(dst_dram, taps, wcols)]
        for g in range(8):
            k.op("gpsimd", lambda gp: gp.memset(src_sb[:, 0:PAD], 0.0))
            k.dma("sync", src_sb[:, PAD:PAD + N],
                  src_dram[g * 128:(g + 1) * 128, 0:N])
            for (dd, tp, wc) in jobs:
                for tau in range(tp):
                    off = PAD - (tp - 1) + tau
                    if tau == 0:
                        k.op("vector", (lambda g=g, off=off, tau=tau, wc=wc, tp=tp:
                                        (lambda v: v.tensor_scalar(
                                            acc[:, :], src_sb[:, off:off + N],
                                            wc[:, g * tp + tau:g * tp + tau + 1],
                                            None, ALU.mult)))())
                    else:
                        k.op("vector", (lambda g=g, off=off, tau=tau, wc=wc, tp=tp:
                                        (lambda v: v.scalar_tensor_tensor(
                                            acc[:, :], src_sb[:, off:off + N],
                                            wc[:, g * tp + tau:g * tp + tau + 1],
                                            acc[:, :], ALU.mult, ALU.add)))())
                if act is not None:
                    k.op("scalar", (lambda: (lambda s: s.activation(
                        ot[:, :], acc[:, :], act)))())
                else:
                    k.op("vector", lambda v: v.tensor_copy(ot[:, :], acc[:, :]))
                k.dma("sync", dd[g * 128:(g + 1) * 128, 0:N], ot[:, :])
        ctx.fence()


def l2_attn(ctx, x_dram, w_q, w_k, w_v, w_b, conv_q, conv_k, conv_v, firs_w,
            gate_w1, gate_b1, gate_w2, gate_b2, norm_w, w_o, x_out, scratch):
    k = ctx.k
    H, DK, C = 4, 256, 32
    NCH = NT // C
    EPSL2, EPSRMS, GEPS = 1e-6, 1e-5, 0.02
    y_dram = scratch["by0"]
    d_dram = scratch["by1"]
    q_dram = scratch["by2"]
    kk_dram = scratch["by3"]
    v_dram = scratch["by4"]
    f8 = scratch["f8"]
    FIRTAPS = [1, 3, 7, 15, 31]

    with contextlib.ExitStack() as S0:
        mu, r, rbc, cbc = ln_pass(ctx, x_dram, y_dram, S0)

    with contextlib.ExitStack() as S1:
        act_sb, act_at = act_loader(ctx, y_dram, D, NT, stack=S1)
        ev = ctx.sb("ev", [128, 512], BF16, stack=S1)

        def mk_evac(dst):
            def evac(b, m0, mw, n0, nw):
                k.op("vector", (lambda b=b, mw=mw, nw=nw: (lambda v: v.tensor_copy(
                    ev[0:mw, 0:nw], b[0:mw, 0:nw])))())
                k.dma("sync", dst[m0:m0 + mw, n0:n0 + nw], ev[0:mw, 0:nw])
            return evac
        gemm(ctx, w_q, D, D, act_at, NT, mk_evac(d_dram), stack=S1, pre=True)
        gemm(ctx, w_k, D, D, act_at, NT, mk_evac(scratch["bys0"]), stack=S1,
             pre=True)
        gemm(ctx, w_v, D, D, act_at, NT, mk_evac(scratch["bys1"]), stack=S1,
             pre=True)
        bsb = ctx.sb("bsb", [H, 512], stack=S1)

        def evac_b(b, m0, mw, n0, nw):
            k.op("scalar", (lambda b=b, mw=mw, nw=nw: (lambda s: s.activation(
                bsb[0:mw, 0:nw], b[0:mw, 0:nw], AF.Sigmoid)))())
            k.dma("sync", f8[m0:m0 + mw, n0:n0 + nw], bsb[0:mw, 0:nw])
        gemm(ctx, w_b, D, H, act_at, NT, evac_b, stack=S1)

    with contextlib.ExitStack() as S2:
        wc_q = ctx.sb("wc_q", [128, 8 * 4], stack=S2)
        wc_k = ctx.sb("wc_k", [128, 8 * 4], stack=S2)
        wc_v = ctx.sb("wc_v", [128, 8 * 4], stack=S2)
        for g in range(8):
            k.dma("sync", wc_q[:, g * 4:(g + 1) * 4], conv_q[g * 128:(g + 1) * 128, :])
            k.dma("sync", wc_k[:, g * 4:(g + 1) * 4], conv_k[g * 128:(g + 1) * 128, :])
            k.dma("sync", wc_v[:, g * 4:(g + 1) * 4], conv_v[g * 128:(g + 1) * 128, :])
        causal_dwconv(ctx, d_dram, q_dram, wc_q, 4, S2, act=AF.Silu)
        causal_dwconv(ctx, scratch["bys0"], kk_dram, wc_k, 4, S2, act=AF.Silu)
        causal_dwconv(ctx, scratch["bys1"], v_dram, wc_v, 4, S2, act=AF.Silu)

    with contextlib.ExitStack() as S3:
        C4 = 4 * C
        cu32 = ctx.sb("cu32", [C, C], stack=S3)
        sl128q = ctx.sb("sl128q", [C4, C4], stack=S3)
        su128q = ctx.sb("su128q", [C4, C4], stack=S3)
        bd4 = ctx.sb("bd4", [C4, C4], stack=S3)
        mco = ctx.sb("mco", [C4, C4], stack=S3)
        k.op("gpsimd", lambda g: g.memset(cu32[:, :], 0.0))
        k.op("gpsimd", lambda g: g.affine_select(
            out=cu32[:, :], in_=cu32[:, :], compare_op=ALU.is_gt, fill=1.0,
            base=0, pattern=[[-1, C]], channel_multiplier=1))
        for (t, cmpo, fill, cm, pat) in (
                (sl128q, ALU.is_ge, -1.0, -1, [[1, C4]]),
                (su128q, ALU.is_ge, -1.0, 1, [[-1, C4]])):
            k.op("gpsimd", (lambda t=t: (lambda g: g.memset(t[:, :], 0.0)))())
            k.op("gpsimd", (lambda t=t, cmpo=cmpo, fill=fill, cm=cm, pat=pat:
                            (lambda g: g.affine_select(
                                out=t[:, :], in_=t[:, :], compare_op=cmpo,
                                fill=fill, base=0, pattern=pat,
                                channel_multiplier=cm)))())
        k.op("gpsimd", lambda g: g.memset(bd4[:, :], 0.0))
        for bq4 in range(4):
            k.op("gpsimd", (lambda: (lambda g: g.memset(mco[:, :], 1.0)))())
            k.op("gpsimd", (lambda bq4=bq4: (lambda g: g.affine_select(
                out=mco[:, :], in_=mco[:, :], compare_op=ALU.is_ge, fill=0.0,
                base=-C * bq4, pattern=[[0, C4]], channel_multiplier=1)))())
            k.op("gpsimd", (lambda bq4=bq4: (lambda g: g.affine_select(
                out=mco[:, :], in_=mco[:, :], compare_op=ALU.is_ge, fill=0.0,
                base=C * bq4 + C - 1, pattern=[[0, C4]], channel_multiplier=-1)))())
            k.op("gpsimd", (lambda bq4=bq4: (lambda g: g.affine_select(
                out=mco[:, :], in_=mco[:, :], compare_op=ALU.is_ge, fill=0.0,
                base=-C * bq4, pattern=[[1, C4]], channel_multiplier=0)))())
            k.op("gpsimd", (lambda bq4=bq4: (lambda g: g.affine_select(
                out=mco[:, :], in_=mco[:, :], compare_op=ALU.is_ge, fill=0.0,
                base=C * bq4 + C - 1, pattern=[[-1, C4]], channel_multiplier=0)))())
            k.op("vector", lambda v: v.tensor_add(bd4[:, :], bd4[:, :], mco[:, :]))
        k.op("vector", lambda v: v.tensor_mul(sl128q[:, :], sl128q[:, :], bd4[:, :]))
        k.op("vector", lambda v: v.tensor_mul(su128q[:, :], su128q[:, :], bd4[:, :]))

        qn = [ctx.sb("l2qn", [128, NT], BF16, stack=S3) for _ in range(2)]
        kn = [ctx.sb("l2kn", [128, NT], BF16, stack=S3) for _ in range(2)]
        kb = [ctx.sb("l2kb", [128, NT], BF16, stack=S3) for _ in range(2)]
        vb2 = [ctx.sb("l2vb", [128, NT], BF16, stack=S3) for _ in range(2)]
        o_dm = [ctx.sb("l2od", [128, NT], BF16, stack=S3) for _ in range(2)]
        ssr = ctx.sb("l2ss", [1, NT], stack=S3)
        sqr = ctx.sb("l2sq", [128, 512], stack=S3)
        rvb = ctx.sb("l2rv", [128, 512], BF16, stack=S3)
        brow_h = ctx.sb("l2br", [1, NT], stack=S3)
        bbc = ctx.sb("l2bbc", [128, NT], BF16, stack=S3)
        B_sb = ctx.sb("l2B", [C4, C4], BF16, stack=S3)
        BT_sb = ctx.sb("l2BT", [C4, C4], BF16, stack=S3)
        X_f = ctx.sb("l2Xf", [C4, 512], F32, stack=S3)
        X_bf = ctx.sb("l2Xb", [C4, 512], BF16, stack=S3)
        Xhi = ctx.sb("l2Xhi", [64, 512], F32, stack=S3)
        wdm = [ctx.sb("l2wdm", [128, C4], BF16, stack=S3) for _ in range(2)]
        ktokA = [ctx.sb("l2ktA", [64, 128], BF16, stack=S3) for _ in range(2)]
        ktokB = [ctx.sb("l2ktB", [64, 128], BF16, stack=S3) for _ in range(2)]
        ktok_c = ctx.sb("l2ktc", [C, 256], BF16, stack=S3)
        attnT = ctx.sb("l2at", [C, C], BF16, stack=S3)
        ua_f = ctx.sb("l2uaf", [C, 256], F32, stack=S3)
        ua_bf = ctx.sb("l2uab", [C, 256], BF16, stack=S3)
        ob_bf = ctx.sb("l2ob", [C, 256], BF16, stack=S3)
        S_f = [ctx.sb("l2Sf", [128, 256], F32, stack=S3) for _ in range(2)]
        S_bf = [ctx.sb("l2Sb", [128, 256], BF16, stack=S3) for _ in range(2)]

        for h in range(H):
            g0, g1 = 2 * h, 2 * h + 1
            for (dst2, srcd) in ((qn, q_dram), (kn, kk_dram)):
                for gg in range(2):
                    k.dma("sync", dst2[gg][:, :],
                          srcd[(2 * h + gg) * 128:(2 * h + gg + 1) * 128, :])
                for n0 in range(0, NT, 512):
                    b = ctx.bank()
                    for gg in range(2):
                        k.op("vector", (lambda dst2=dst2, gg=gg, n0=n0:
                                        (lambda v: v.tensor_mul(
                                            sqr[:, :], dst2[gg][:, n0:n0 + 512],
                                            dst2[gg][:, n0:n0 + 512])))())
                        k.op("tensor", (lambda b=b, s=(gg == 0), e=(gg == 1):
                                        (lambda t: t.matmul(
                                            b[0:1, 0:512], ctx.ones_col[:, :],
                                            sqr[:, :], start=s, stop=e)))())
                    k.op("vector", (lambda b=b: (lambda v: v.tensor_scalar(
                        sqr[0:1, 0:512], b[0:1, 0:512], EPSL2, None, ALU.add)))())
                    k.op("scalar", (lambda n0=n0: (lambda s: s.activation(
                        ssr[0:1, n0:n0 + 512], sqr[0:1, 0:512], AF.Sqrt)))())
                k.op("vector", lambda v: v.reciprocal(ssr[0:1, :], ssr[0:1, :]))
                for n0 in range(0, NT, 512):
                    b = ctx.bank()
                    k.op("tensor", (lambda b=b, n0=n0: (lambda t: t.matmul(
                        b[:, :], ctx.ones_row[0:1, :], ssr[0:1, n0:n0 + 512],
                        start=True, stop=True)))())
                    k.op("vector", (lambda b=b: (lambda v: v.tensor_copy(
                        rvb[:, :], b[:, :])))())
                    for gg in range(2):
                        k.op("vector", (lambda dst2=dst2, gg=gg, n0=n0:
                                        (lambda v: v.tensor_mul(
                                            dst2[gg][:, n0:n0 + 512],
                                            dst2[gg][:, n0:n0 + 512], rvb[:, :])))())
            k.dma("sync", brow_h[0:1, :], f8[h:h + 1, :])
            for n0 in range(0, NT, 512):
                b = ctx.bank()
                k.op("tensor", (lambda b=b, n0=n0: (lambda t: t.matmul(
                    b[:, :], ctx.ones_row[0:1, :], brow_h[0:1, n0:n0 + 512],
                    start=True, stop=True)))())
                k.op("vector", (lambda b=b, n0=n0: (lambda v: v.tensor_copy(
                    bbc[:, n0:n0 + 512], b[:, :])))())
            for gg in range(2):
                k.dma("sync", vb2[gg][:, :],
                      v_dram[(2 * h + gg) * 128:(2 * h + gg + 1) * 128, :])
                k.op("vector", (lambda gg=gg: (lambda v: v.tensor_mul(
                    kb[gg][:, :], kn[gg][:, :], bbc[:, :])))())
                k.op("vector", (lambda gg=gg: (lambda v: v.tensor_mul(
                    vb2[gg][:, :], vb2[gg][:, :], bbc[:, :])))())
                k.op("vector", (lambda gg=gg: (lambda v: v.memset(
                    S_f[gg][:, :], 0.0)))())
                k.op("vector", (lambda gg=gg: (lambda v: v.memset(
                    S_bf[gg][:, :], 0.0)))())

            for qi in range(NCH // 4):
                q0 = qi * C4
                b1 = ctx.bank()
                b2 = ctx.bank()
                for gg in range(2):
                    k.op("tensor", (lambda b1=b1, gg=gg, q0=q0, s=(gg == 0),
                                    e=(gg == 1): (lambda t: t.matmul(
                                        b1[0:C4, 0:C4], kb[gg][:, q0:q0 + C4],
                                        kn[gg][:, q0:q0 + C4], start=s, stop=e)))())
                for gg in range(2):
                    k.op("tensor", (lambda b2=b2, gg=gg, q0=q0, s=(gg == 0),
                                    e=(gg == 1): (lambda t: t.matmul(
                                        b2[0:C4, 0:C4], kn[gg][:, q0:q0 + C4],
                                        kb[gg][:, q0:q0 + C4], start=s, stop=e)))())
                k.op("vector", (lambda b1=b1: (lambda v: v.tensor_mul(
                    B_sb[:, :], b1[0:C4, 0:C4], sl128q[:, :])))())
                k.op("vector", (lambda b2=b2: (lambda v: v.tensor_mul(
                    BT_sb[:, :], b2[0:C4, 0:C4], su128q[:, :])))())
                for gg in range(2):
                    k.op("tensor", (lambda gg=gg, q0=q0: (lambda t: t.transpose(
                        ctx.ps_tr[0:C4, 0:128], vb2[gg][:, q0:q0 + C4],
                        ctx.ident[:, :])))())
                    k.op("vector", (lambda gg=gg: (lambda v: v.tensor_copy(
                        X_bf[:, gg * 128:(gg + 1) * 128],
                        ctx.ps_tr[0:C4, 0:128])))())
                    k.op("vector", (lambda gg=gg: (lambda v: v.tensor_copy(
                        X_f[:, gg * 128:(gg + 1) * 128],
                        ctx.ps_tr[0:C4, 0:128])))())
                    k.op("tensor", (lambda gg=gg, q0=q0: (lambda t: t.transpose(
                        ctx.ps_tr[0:C4, 128:256], kb[gg][:, q0:q0 + C4],
                        ctx.ident[:, :])))())
                    k.op("vector", (lambda gg=gg: (lambda v: v.tensor_copy(
                        X_bf[:, 256 + gg * 128:256 + (gg + 1) * 128],
                        ctx.ps_tr[0:C4, 128:256])))())
                    k.op("vector", (lambda gg=gg: (lambda v: v.tensor_copy(
                        X_f[:, 256 + gg * 128:256 + (gg + 1) * 128],
                        ctx.ps_tr[0:C4, 128:256])))())
                for it in range(5):
                    b3 = ctx.bank()
                    k.op("tensor", (lambda b3=b3: (lambda t: t.matmul(
                        b3[0:C4, 0:512], BT_sb[:, :], X_bf[:, :],
                        start=True, stop=True)))())
                    k.op("vector", (lambda b3=b3: (lambda v: v.tensor_add(
                        X_f[:, :], X_f[:, :], b3[0:C4, 0:512])))())
                    k.op("vector", lambda v: v.tensor_copy(X_bf[:, :], X_f[:, :]))
                    if it < 4:
                        b4 = ctx.bank()
                        b5 = ctx.bank()
                        k.op("tensor", (lambda b4=b4: (lambda t: t.matmul(
                            b4[0:C4, 0:C4], BT_sb[:, :], B_sb[:, :],
                            start=True, stop=True)))())
                        k.op("tensor", (lambda b5=b5: (lambda t: t.matmul(
                            b5[0:C4, 0:C4], B_sb[:, :], BT_sb[:, :],
                            start=True, stop=True)))())
                        k.op("vector", (lambda b4=b4: (lambda v: v.tensor_copy(
                            B_sb[:, :], b4[0:C4, 0:C4])))())
                        k.op("vector", (lambda b5=b5: (lambda v: v.tensor_copy(
                            BT_sb[:, :], b5[0:C4, 0:C4])))())
                k.op("vector", lambda v: v.tensor_copy(Xhi[:, :], X_f[64:128, :]))
                for gg in range(2):
                    k.op("tensor", (lambda gg=gg: (lambda t: t.transpose(
                        ctx.ps_tr[0:128, 0:C4],
                        X_bf[:, 256 + gg * 128:256 + (gg + 1) * 128],
                        ctx.ident[0:C4, 0:C4])))())
                    k.op("vector", (lambda gg=gg: (lambda v: v.tensor_copy(
                        wdm[gg][:, :], ctx.ps_tr[0:128, 0:C4])))())
                    k.op("tensor", (lambda gg=gg, q0=q0: (lambda t: t.transpose(
                        ctx.ps_tr[0:64, 0:128], kn[gg][:, q0:q0 + 64],
                        ctx.ident[:, :])))())
                    k.op("vector", (lambda gg=gg: (lambda v: v.tensor_copy(
                        ktokA[gg][:, :], ctx.ps_tr[0:64, 0:128])))())
                    k.op("tensor", (lambda gg=gg, q0=q0: (lambda t: t.transpose(
                        ctx.ps_tr[0:64, 128:256], kn[gg][:, q0 + 64:q0 + 128],
                        ctx.ident[:, :])))())
                    k.op("vector", (lambda gg=gg: (lambda v: v.tensor_copy(
                        ktokB[gg][:, :], ctx.ps_tr[0:64, 128:256])))())
                for j in range(4):
                    cc = q0 + j * C
                    ba = ctx.bank()
                    for gg in range(2):
                        k.op("tensor", (lambda ba=ba, gg=gg, cc=cc, s=(gg == 0),
                                        e=(gg == 1): (lambda t: t.matmul(
                                            ba[0:C, 0:C], kn[gg][:, cc:cc + C],
                                            qn[gg][:, cc:cc + C],
                                            start=s, stop=e)))())
                    k.op("vector", (lambda ba=ba: (lambda v: v.tensor_mul(
                        attnT[:, :], ba[0:C, 0:C], cu32[:, :])))())
                    src = ktokA if j < 2 else ktokB
                    r0 = (j % 2) * C
                    for gg in range(2):
                        k.op("vector", (lambda src=src, gg=gg, r0=r0:
                                        (lambda v: v.tensor_copy(
                                            ktok_c[:, gg * 128:(gg + 1) * 128],
                                            src[gg][r0:r0 + C, :])))())
                    bu = ctx.bank()
                    for gg in range(2):
                        k.op("tensor", (lambda bu=bu, gg=gg, j=j, s=(gg == 0),
                                        e=(gg == 1): (lambda t: t.matmul(
                                            bu[0:C, 0:256],
                                            wdm[gg][:, j * C:(j + 1) * C],
                                            S_bf[gg][:, :], start=s, stop=e)))())
                    if j < 2:
                        usrc = X_f
                        ur = j * C
                    else:
                        usrc = Xhi
                        ur = (j - 2) * C
                    k.op("vector", (lambda bu=bu, usrc=usrc, ur=ur:
                                    (lambda v: v.tensor_sub(
                                        ua_f[:, :], usrc[ur:ur + C, 0:256],
                                        bu[0:C, 0:256])))())
                    k.op("vector", lambda v: v.tensor_copy(ua_bf[:, :], ua_f[:, :]))
                    bo = ctx.bank()
                    for gg in range(2):
                        k.op("tensor", (lambda bo=bo, gg=gg, cc=cc, s=(gg == 0):
                                        (lambda t: t.matmul(
                                            bo[0:C, 0:256], qn[gg][:, cc:cc + C],
                                            S_bf[gg][:, :], start=s,
                                            stop=False)))())
                    k.op("tensor", (lambda bo=bo: (lambda t: t.matmul(
                        bo[0:C, 0:256], attnT[:, :], ua_bf[:, :],
                        start=False, stop=True)))())
                    k.op("vector", (lambda bo=bo: (lambda v: v.tensor_copy(
                        ob_bf[:, :], bo[0:C, 0:256])))())
                    for gg in range(2):
                        k.op("tensor", (lambda gg=gg: (lambda t: t.transpose(
                            ctx.ps_tr[0:128, 0:C], ob_bf[:, gg * 128:(gg + 1) * 128],
                            ctx.ident[0:C, 0:C])))())
                        k.op("vector", (lambda gg=gg, cc=cc: (lambda v: v.tensor_copy(
                            o_dm[gg][:, cc:cc + C], ctx.ps_tr[0:128, 0:C])))())
                    for gg in range(2):
                        bs2 = ctx.bank()
                        k.op("tensor", (lambda bs2=bs2, gg=gg: (lambda t: t.matmul(
                            bs2[0:128, 0:256], ktok_c[:, gg * 128:(gg + 1) * 128],
                            ua_bf[:, :], start=True, stop=True)))())
                        k.op("vector", (lambda bs2=bs2, gg=gg: (lambda v: v.tensor_add(
                            S_f[gg][:, :], S_f[gg][:, :], bs2[0:128, 0:256])))())
                        k.op("vector", (lambda gg=gg: (lambda v: v.tensor_copy(
                            S_bf[gg][:, :], S_f[gg][:, :])))())
            for gg in range(2):
                k.dma("sync", d_dram[(2 * h + gg) * 128:(2 * h + gg + 1) * 128, :],
                      o_dm[gg][:, :])

    with contextlib.ExitStack() as S4:
        wcf = []
        for i, tp in enumerate(FIRTAPS):
            wc = ctx.sb(f"wcf{i}", [128, 8 * tp], stack=S4)
            for g in range(8):
                k.dma("sync", wc[:, g * tp:(g + 1) * tp],
                      firs_w[i][g * 128:(g + 1) * 128, :])
            wcf.append(wc)
        jobs = [(scratch[f"bys{i}"], FIRTAPS[i], wcf[i]) for i in range(5)]
        causal_dwconv(ctx, v_dram, None, None, None, S4, accum_dsts=jobs)
        ab = ctx.sb("st_ab", [128, 512], stack=S4)
        srow = ctx.sb("st_r", [1, 512], stack=S4)
        streams = [scratch[f"bys{i}"] for i in range(5)] + [d_dram, v_dram]
        for si, sd in enumerate(streams):
            for h in range(H):
                for n0 in range(0, NT, 512):
                    b = ctx.bank()
                    for gg in range(2):
                        k.dma("gpsimd", ab[:, :],
                              sd[(2 * h + gg) * 128:(2 * h + gg + 1) * 128,
                                 n0:n0 + 512])
                        k.op("scalar", lambda s: s.activation(ab[:, :], ab[:, :],
                                                              AF.Abs))
                        k.op("tensor", (lambda b=b, s=(gg == 0), e=(gg == 1):
                                        (lambda t: t.matmul(
                                            b[0:1, 0:512], ctx.ones_col[:, :],
                                            ab[:, :], start=s, stop=e)))())
                    k.op("vector", (lambda b=b: (lambda v: v.tensor_scalar(
                        srow[0:1, :], b[0:1, 0:512], 1.0 / 256.0, None, ALU.mult)))())
                    k.dma("sync", f8[8 + si * H + h:8 + si * H + h + 1, n0:n0 + 512],
                          srow[0:1, :])

    with contextlib.ExitStack() as S5:
        kg = 9
        act2 = ctx.sb("g_act", [128, kg * NT], BF16, stack=S5)
        for g in range(8):
            k.dma("sync", act2[0:128, g * NT:(g + 1) * NT],
                  y_dram[g * 128:(g + 1) * 128, :])
        k.dma("gpsimd", act2[0:28, 8 * NT:9 * NT], f8[8:36, :])

        def at2(g, n0, nw):
            return act2[0:128, g * NT + n0:g * NT + n0 + nw]
        gb1 = ctx.sb("g_b1", [128, 16], stack=S5)
        k.dma("sync", gb1[:, :], gate_b1)
        ev2 = ctx.sb("g_ev", [128, 512], BF16, stack=S5)

        def evac_h(b, m0, mw, n0, nw):
            k.op("scalar", (lambda b=b, mw=mw, nw=nw, mi=m0 // 128:
                            (lambda s: s.activation(
                                ev2[0:mw, 0:nw], b[0:mw, 0:nw], AF.Gelu,
                                bias=gb1[:, mi:mi + 1])))())
            k.dma("sync", scratch["bh0"][m0:m0 + mw, n0:n0 + nw], ev2[0:mw, 0:nw])
        gemm(ctx, gate_w1, 1052, 2048, at2, NT, evac_h, stack=S5)

    with contextlib.ExitStack() as S6:
        act3, at3 = act_loader(ctx, scratch["bh0"], 2048, NT, stack=S6)
        gb2 = ctx.sb("g_b2", [28, 1], stack=S6)
        k.dma("sync", gb2[:, :], gate_b2)
        logit_sb = ctx.sb("g_lg", [28, NT], stack=S6)

        def evac_l(b, m0, mw, n0, nw):
            k.op("vector", (lambda b=b, mw=mw, n0=n0, nw=nw:
                            (lambda v: v.tensor_scalar(
                                logit_sb[0:mw, n0:n0 + nw], b[0:mw, 0:nw],
                                gb2[0:mw, :], None, ALU.add)))())
        gemm(ctx, gate_w2, 2048, 28, at3, NT, evac_l, stack=S6)
        tm = ctx.sb("w_tm", [128, 28], stack=S6)
        ab2 = ctx.sb("w_ab", [128, 28], stack=S6)
        ex = ctx.sb("w_ex", [128, 28], stack=S6)
        rl = ctx.sb("w_rl", [128, 28], stack=S6)
        sm = ctx.sb("w_sm", [128, 4], stack=S6)
        for t0 in range(0, NT, 128):
            bp = ctx.ps_aux
            k.op("tensor", (lambda t0=t0: (lambda t: t.transpose(
                bp[0:128, 0:28], logit_sb[0:28, t0:t0 + 128],
                ctx.ident_f[0:28, 0:28])))())
            k.op("vector", lambda v: v.tensor_copy(tm[:, :], bp[0:128, 0:28]))
            k.op("scalar", lambda s: s.activation(ab2[:, :], tm[:, :], AF.Abs))
            k.op("scalar", lambda s: s.activation(ex[:, :], ab2[:, :], AF.Exp,
                                                  scale=-1.0))
            k.op("scalar", lambda s: s.activation(ex[:, :], ex[:, :], AF.Ln,
                                                  bias=ctx.ones_col[:, 0:1]))
            k.op("vector", lambda v: v.tensor_scalar(rl[:, :], tm[:, :], 0.0,
                                                     None, ALU.max))
            k.op("vector", lambda v: v.tensor_add(rl[:, :], rl[:, :], ex[:, :]))
            k.op("vector", lambda v: v.tensor_scalar(rl[:, :], rl[:, :], GEPS,
                                                     None, ALU.add))
            k.op("vector", lambda v: v.tensor_reduce(
                sm[:, :], rl[:, :].rearrange("p (h s) -> p h s", s=7), AX.X,
                ALU.add))
            k.op("vector", lambda v: v.reciprocal(sm[:, :], sm[:, :]))
            for h in range(4):
                k.op("vector", (lambda h=h: (lambda v: v.tensor_scalar(
                    rl[:, h * 7:(h + 1) * 7], rl[:, h * 7:(h + 1) * 7],
                    sm[:, h:h + 1], None, ALU.mult)))())
            k.op("tensor", (lambda t0=t0: (lambda t: t.transpose(
                bp[0:28, 0:128], rl[0:128, 0:28], ctx.ident_f[:, :])))())
            k.op("vector", (lambda t0=t0: (lambda v: v.tensor_copy(
                logit_sb[0:28, t0:t0 + 128], bp[0:28, 0:128])))())
        k.dma("sync", f8[36:64, :], logit_sb[0:28, :])

    with contextlib.ExitStack() as S7:
        wrow = ctx.sb("c_wr", [1, NT], stack=S7)
        wb_c = ctx.sb("c_wb", [128, 512], stack=S7)
        stile = ctx.sb("c_st", [128, 512], BF16, stack=S7)
        oacc = [ctx.sb("c_oa", [128, 512], stack=S7) for _ in range(2)]
        tmp7 = ctx.sb("c_tp", [128, 512], stack=S7)
        ssq = ctx.sb("c_ssq", [1, 512], stack=S7)
        rsb = ctx.sb("c_rs", [128, 512], stack=S7)
        nw_c = ctx.sb("c_nw", [128, 2], stack=S7)
        onrm = ctx.sb("c_on", [128, 512], BF16, stack=S7)
        k.dma("sync", nw_c[:, :], norm_w)
        streams = [scratch[f"bys{i}"] for i in range(5)] + [d_dram, v_dram]
        for h in range(H):
            for n0 in range(0, NT, 512):
                for gg in range(2):
                    k.op("vector", (lambda gg=gg: (lambda v: v.memset(
                        oacc[gg][:, :], 0.0)))())
                for si, sd in enumerate(streams):
                    k.dma("sync", wrow[0:1, 0:512],
                          f8[36 + h * 7 + si:36 + h * 7 + si + 1, n0:n0 + 512])
                    b = ctx.bank()
                    k.op("tensor", (lambda b=b: (lambda t: t.matmul(
                        b[:, :], ctx.ones_row[0:1, :], wrow[0:1, 0:512],
                        start=True, stop=True)))())
                    k.op("vector", (lambda b=b: (lambda v: v.tensor_copy(
                        wb_c[:, :], b[:, :])))())
                    for gg in range(2):
                        k.dma("sync", stile[:, :],
                              sd[(2 * h + gg) * 128:(2 * h + gg + 1) * 128,
                                 n0:n0 + 512])
                        k.op("vector", lambda v: v.tensor_mul(tmp7[:, :],
                                                              stile[:, :], wb_c[:, :]))
                        k.op("vector", (lambda gg=gg: (lambda v: v.tensor_add(
                            oacc[gg][:, :], oacc[gg][:, :], tmp7[:, :])))())
                b = ctx.bank()
                for gg in range(2):
                    k.op("vector", (lambda gg=gg: (lambda v: v.tensor_mul(
                        tmp7[:, :], oacc[gg][:, :], oacc[gg][:, :])))())
                    k.op("tensor", (lambda b=b, s=(gg == 0), e=(gg == 1):
                                    (lambda t: t.matmul(
                                        b[0:1, 0:512], ctx.ones_col[:, :],
                                        tmp7[:, :], start=s, stop=e)))())
                k.op("vector", (lambda b=b: (lambda v: v.tensor_scalar(
                    ssq[0:1, :], b[0:1, 0:512], 1.0 / 256.0, EPSRMS,
                    ALU.mult, ALU.add)))())
                k.op("scalar", lambda s: s.activation(ssq[0:1, :], ssq[0:1, :],
                                                      AF.Sqrt))
                k.op("vector", lambda v: v.reciprocal(ssq[0:1, :], ssq[0:1, :]))
                b = ctx.bank()
                k.op("tensor", (lambda b=b: (lambda t: t.matmul(
                    b[:, :], ctx.ones_row[0:1, :], ssq[0:1, 0:512],
                    start=True, stop=True)))())
                k.op("vector", (lambda b=b: (lambda v: v.tensor_copy(
                    rsb[:, :], b[:, :])))())
                for gg in range(2):
                    k.op("vector", (lambda gg=gg: (lambda v: v.tensor_mul(
                        tmp7[:, :], oacc[gg][:, :], rsb[:, :])))())
                    k.op("vector", (lambda gg=gg: (lambda v: v.tensor_scalar(
                        onrm[:, :], tmp7[:, :], nw_c[:, gg:gg + 1], None,
                        ALU.mult)))())
                    k.dma("sync", kk_dram[(2 * h + gg) * 128:(2 * h + gg + 1) * 128,
                                          n0:n0 + 512], onrm[:, :])

    with contextlib.ExitStack() as S8:
        act4, at4 = act_loader(ctx, kk_dram, D, NT, stack=S8)
        xs8 = ctx.sb("f_x", [128, 512], stack=S8)
        t8 = ctx.sb("f_t", [128, 512], stack=S8)

        def evac_f(b, m0, mw, n0, nw):
            k.dma("sync", xs8[0:mw, 0:nw], x_dram[m0:m0 + mw, n0:n0 + nw])
            k.op("vector", (lambda b=b, mw=mw, nw=nw: (lambda v: v.tensor_add(
                t8[0:mw, 0:nw], xs8[0:mw, 0:nw], b[0:mw, 0:nw])))())
            k.dma("sync", x_out[m0:m0 + mw, n0:n0 + nw], t8[0:mw, 0:nw])
        gemm(ctx, w_o, D, D, at4, NT, evac_f, stack=S8, pre=True)



def l2_ffn(ctx, x_dram, gate_wT, w1, b1, w2, b2, sw1, sb1, sw2, sb2, x_out,
           scratch):
    k = ctx.k
    HID = 4096
    y_dram = scratch["by0"]
    f8 = scratch["f8"]

    with contextlib.ExitStack() as SC:
        m0row = ctx.sb("m0r", [1, NT], stack=SC)
        with contextlib.ExitStack() as S0:
            mu, r, rbc, cbc = ln_pass(ctx, x_dram, y_dram, S0)
            logits_sb = ctx.sb("lgt2", [2, NT], stack=S0)
            gw_sb = ctx.sb("gw2", [128, 8 * 2], stack=S0)
            for g in range(8):
                k.dma("sync", gw_sb[:, g * 2:(g + 1) * 2],
                      gate_wT[g * 128:(g + 1) * 128, :])
            fp32_logits(ctx, x_dram, rbc, cbc,
                        lambda g: gw_sb[:, g * 2:(g + 1) * 2], 2, logits_sb,
                        None, S0)
            k.dma("sync", f8[0:2, :], logits_sb[0:2, :])
        l0r = ctx.sb("l0r", [1, NT], stack=SC)
        l1r = ctx.sb("l1r", [1, NT], stack=SC)
        k.dma("sync", l0r[0:1, :], f8[0:1, :])
        k.dma("sync", l1r[0:1, :], f8[1:2, :])
        k.op("vector", lambda v: v.tensor_tensor(m0row[0:1, :], l0r[0:1, :],
                                                 l1r[0:1, :], ALU.is_ge))

        dead = [ctx.sb("dead", [128, 8], stack=SC) for _ in range(2)]
        with contextlib.ExitStack() as S1:
            rb1 = ctx.sb("rb1", [128, 32], stack=S1)
            wcol = ctx.sb("dwc", [128, 128], BF16, stack=S1)
            rb1b = ctx.sb("rb1b", [128, 32], BF16, stack=S1)
            for e in range(2):
                k.dma("sync", rb1[:, :], b1[e])
                k.op("vector", lambda v: v.tensor_scalar(rb1[:, :], rb1[:, :],
                                                         0.0, None, ALU.max))
                k.op("vector", lambda v: v.tensor_copy(rb1b[:, :], rb1[:, :]))
                for f0 in range(8):
                    bq = ctx.bank()
                    for g in range(32):
                        k.dma("gpsimd", wcol[:, :],
                              w2[e, f0 * 128:(f0 + 1) * 128, g * 128:(g + 1) * 128])
                        k.op("tensor", (lambda bq=bq, g=g, s=(g == 0), e2=(g == 31):
                                        (lambda t: t.matmul(
                                            bq[0:128, 0:1].transpose([0, 1]) if False
                                            else bq[0:128, 0:1],
                                            wcol[:, :], rb1b[:, g:g + 1],
                                            start=s, stop=e2)))())
                    k.op("vector", (lambda bq=bq, e=e, f0=f0: (lambda v: v.tensor_copy(
                        dead[e][:, f0:f0 + 1], bq[0:128, 0:1])))())
                bcol = ctx.sb("dbc", [128, 8], stack=S1)
                k.dma("sync", bcol[:, :], b2[e])
                k.op("vector", (lambda e=e, bcol=bcol: (lambda v: v.tensor_add(
                    dead[e][:, :], dead[e][:, :], bcol[:, :])))())

        with contextlib.ExitStack() as S2:
            NQ = 1024
            act_sb = ctx.sb("r_y", [128, 8 * NQ], BF16, stack=S2)
            g_buf = ctx.sb("r_g", [128, 32 * NQ], BF16, stack=S2)
            acc = ctx.sb("r_acc", [128, 8 * NQ], F32, stack=S2)
            m0bc = ctx.sb("r_m0", [128, NQ], F32, stack=S2)
            m1bc = ctx.sb("r_m1", [128, NQ], F32, stack=S2)
            b1c = ctx.sb("r_b1", [128, 32], stack=S2)
            b2c = ctx.sb("r_b2", [128, 8], stack=S2)
            t2 = ctx.sb("r_t2", [128, 512], F32, stack=S2)
            xr = ctx.sb("r_x", [128, 512], F32, stack=S2)

            units = [(sw1, sb1, sw2, sb2, None)] + \
                    [(w1[e], b1[e], w2[e], b2[e], e) for e in range(2)]
            for nq in range(4):
                nbase = nq * NQ
                atok = None
                for g in range(8):
                    atok = k.dma("sync", act_sb[:, g * NQ:(g + 1) * NQ],
                                 y_dram[g * 128:(g + 1) * 128, nbase:nbase + NQ])
                bcast_row(ctx, m0row[0:1, nbase:nbase + NQ], m0bc, NQ)
                for n0 in range(0, NQ, 512):
                    k.op("vector", (lambda n0=n0: (lambda v: v.tensor_scalar(
                        m1bc[:, n0:n0 + 512], m0bc[:, n0:n0 + 512], -1.0, 1.0,
                        ALU.mult, ALU.add)))())
                at1 = lambda g, n0, nw: act_sb[:, g * NQ + n0:g * NQ + n0 + nw]
                at2 = lambda g, n0, nw: g_buf[:, g * NQ + n0:g * NQ + n0 + nw]
                for ui, (uw1, ub1, uw2, ub2, ue) in enumerate(units):
                    k.dma("sync", b1c[:, :], ub1)
                    k.dma("sync", b2c[:, :], ub2)

                    def evac_g(b, m0, mw, n0, nw):
                        k.op("scalar", (lambda b=b, m0=m0, mw=mw, n0=n0, nw=nw:
                                        (lambda s: s.activation(
                                            g_buf[:, (m0 // 128) * NQ + n0:
                                                  (m0 // 128) * NQ + n0 + nw],
                                            b[0:mw, 0:nw], AF.Relu,
                                            bias=b1c[:, m0 // 128:m0 // 128 + 1])))())
                    gemm(ctx, uw1, D, HID, at1, NQ, evac_g, stack=S2,
                         act_tok=atok if ui == 0 else None, pre=True)
                    s1tok = k.last

                    def evac_o(b, m0, mw, n0, nw, ue=ue):
                        dst = acc[:, (m0 // 128) * NQ + n0:(m0 // 128) * NQ + n0 + nw]
                        k.op("vector", (lambda b=b, m0=m0, mw=mw, nw=nw:
                                        (lambda v: v.tensor_scalar(
                                            t2[:, 0:nw], b[0:mw, 0:nw],
                                            b2c[:, m0 // 128:m0 // 128 + 1], None,
                                            ALU.add)))())
                        if ue is None:
                            k.op("vector", (lambda dst=dst, nw=nw:
                                            (lambda v: v.tensor_copy(
                                                dst, t2[:, 0:nw])))())
                        else:
                            mb = m0bc if ue == 0 else m1bc
                            k.op("vector", (lambda mb=mb, n0=n0, nw=nw:
                                            (lambda v: v.tensor_mul(
                                                t2[:, 0:nw], t2[:, 0:nw],
                                                mb[:, n0:n0 + nw])))())
                            k.op("vector", (lambda dst=dst, nw=nw:
                                            (lambda v: v.tensor_add(
                                                dst, dst, t2[:, 0:nw])))())
                    gemm(ctx, uw2, HID, D, at2, NQ, evac_o, stack=S2,
                         act_tok=s1tok, pre=True)
                for g in range(8):
                    for ni in range(2):
                        n0 = ni * 512
                        k.dma("sync", xr[:, :],
                              x_dram[g * 128:(g + 1) * 128, nbase + n0:nbase + n0 + 512])
                        k.op("vector", (lambda g=g, n0=n0: (lambda v: v.tensor_scalar(
                            t2[:, :], m0bc[:, n0:n0 + 512], dead[1][:, g:g + 1],
                            None, ALU.mult)))())
                        k.op("vector", (lambda g=g, n0=n0: (lambda v: v.scalar_tensor_tensor(
                            t2[:, :], m1bc[:, n0:n0 + 512], dead[0][:, g:g + 1],
                            t2[:, :], ALU.mult, ALU.add)))())
                        k.op("vector", lambda v: v.tensor_add(t2[:, :], t2[:, :],
                                                              xr[:, :]))
                        k.op("vector", (lambda g=g, n0=n0: (lambda v: v.tensor_add(
                            t2[:, :], t2[:, :],
                            acc[:, g * NQ + n0:g * NQ + n0 + 512])))())
                        k.dma("sync", x_out[g * 128:(g + 1) * 128,
                                            nbase + n0:nbase + n0 + 512], t2[:, :])



T3 = 512
R3 = 8


def l3_attn(ctx, xs_dram, wqT, bq_c, wkT, bk_c, wvT, bv_c, x_out, scratch,
            gamma=0.5):
    k = ctx.k
    y3 = scratch["l3y"]
    q3 = scratch["l3q"]
    k3 = scratch["l3k"]
    v3 = scratch["l3v"]
    f8 = scratch["f8"]

    with contextlib.ExitStack() as S0:
        for rr in range(R3):
            with contextlib.ExitStack() as SL:
                ln_pass(ctx, xs_dram[rr * 1024:(rr + 1) * 1024, :],
                        y3[rr * 1024:(rr + 1) * 1024, :], SL, N=T3)
        with contextlib.ExitStack() as S1:
            bqs = ctx.sb("l3bq", [128, 8], stack=S1)
            bks = ctx.sb("l3bk", [128, 8], stack=S1)
            bvs = ctx.sb("l3bv", [128, 8], stack=S1)
            k.dma("sync", bqs[:, :], bq_c)
            k.dma("sync", bks[:, :], bk_c)
            k.dma("sync", bvs[:, :], bv_c)
            ev = ctx.sb("l3ev", [128, 512], BF16, stack=S1)
            for rr in range(R3):
                act_stack = contextlib.ExitStack()
                with act_stack as SA:
                    act_sb, act_at = act_loader(
                        ctx, y3[rr * 1024:(rr + 1) * 1024, :], D, T3, stack=SA)
                    for (wT, bcol, dst) in ((wqT, bqs, q3), (wkT, bks, k3),
                                            (wvT, bvs, v3)):
                        def evac(b, m0, mw, n0, nw, bcol=bcol, dst=dst, rr=rr):
                            k.op("vector", (lambda b=b, mw=mw, nw=nw, mi=m0 // 128,
                                            bcol=bcol: (lambda v: v.tensor_scalar(
                                                ev[0:mw, 0:nw], b[0:mw, 0:nw],
                                                bcol[:, mi:mi + 1], None, ALU.add)))())
                            k.dma("sync", dst[rr * 1024 + m0:rr * 1024 + m0 + mw,
                                              n0:n0 + nw], ev[0:mw, 0:nw])
                        gemm(ctx, wT, D, D, act_at, T3, evac, stack=SA,
                             pre=True)

        with contextlib.ExitStack() as S2:
            qsb = ctx.sb("l3qs", [128, 8 * T3], BF16, stack=S2)
            kall = ctx.sb("l3ka", [128, 8 * R3 * T3], BF16, stack=S2)
            vall = ctx.sb("l3va", [128, 8 * R3 * T3], BF16, stack=S2)
            sq3 = ctx.sb("l3sq", [128, 512], stack=S2)
            q2r = ctx.sb("l3q2", [R3, T3], stack=S2)
            k2r = ctx.sb("l3k2", [R3, T3], stack=S2)
            qkr = ctx.sb("l3qk", [R3, T3], stack=S2)
            tmq = ctx.sb("l3tq", [128, R3], stack=S2)
            tmk = ctx.sb("l3tk", [128, R3], stack=S2)
            tm2 = ctx.sb("l3t2", [128, R3], stack=S2)
            mncol = ctx.sb("l3mn", [128, 1], stack=S2)
            smcol = ctx.sb("l3sm", [128, 1], stack=S2)
            simr = ctx.sb("l3si", [R3, T3], stack=S2)
            srow0 = ctx.sb("l3s0", [1, T3], stack=S2)
            srow = ctx.sb("l3sr", [1, T3], stack=S2)
            sbc = ctx.sb("l3sb", [128, T3], stack=S2)
            accv = ctx.sb("l3ac", [128, 8 * T3], F32, stack=S2)
            xr3 = ctx.sb("l3xr", [128, 512], stack=S2)
            for j in range(R3):
                for g in range(8):
                    k.dma("sync", kall[:, (g * R3 + j) * T3:(g * R3 + j + 1) * T3],
                          k3[j * 1024 + g * 128:j * 1024 + (g + 1) * 128, :])
                    k.dma("sync", vall[:, (g * R3 + j) * T3:(g * R3 + j + 1) * T3],
                          v3[j * 1024 + g * 128:j * 1024 + (g + 1) * 128, :])
                b = ctx.bank()
                for g in range(8):
                    k.op("vector", (lambda j=j, g=g: (lambda v: v.tensor_mul(
                        sq3[:, :], kall[:, (g * R3 + j) * T3:(g * R3 + j + 1) * T3],
                        kall[:, (g * R3 + j) * T3:(g * R3 + j + 1) * T3])))())
                    k.op("tensor", (lambda b=b, s=(g == 0), e=(g == 7):
                                    (lambda t: t.matmul(
                                        b[0:1, 0:T3], ctx.ones_col[:, :],
                                        sq3[:, :], start=s, stop=e)))())
                k.op("vector", (lambda b=b: (lambda v: v.tensor_copy(
                    srow0[0:1, :], b[0:1, 0:T3])))())
                k.dma("sync", k2r[j:j + 1, :], srow0[0:1, :])
            for rr in range(R3):
                for g in range(8):
                    k.dma("sync", qsb[:, g * T3:(g + 1) * T3],
                          q3[rr * 1024 + g * 128:rr * 1024 + (g + 1) * 128, :])
                b = ctx.bank()
                for g in range(8):
                    k.op("vector", (lambda g=g: (lambda v: v.tensor_mul(
                        sq3[:, :], qsb[:, g * T3:(g + 1) * T3],
                        qsb[:, g * T3:(g + 1) * T3])))())
                    k.op("tensor", (lambda b=b, s=(g == 0), e=(g == 7):
                                    (lambda t: t.matmul(
                                        b[0:1, 0:T3], ctx.ones_col[:, :],
                                        sq3[:, :], start=s, stop=e)))())
                k.op("vector", (lambda b=b: (lambda v: v.tensor_copy(
                    q2r[0:1, :], b[0:1, 0:T3])))())
                for j in range(R3):
                    b = ctx.bank()
                    for g in range(8):
                        k.op("vector", (lambda j=j, g=g: (lambda v: v.tensor_mul(
                            sq3[:, :], qsb[:, g * T3:(g + 1) * T3],
                            kall[:, (g * R3 + j) * T3:(g * R3 + j + 1) * T3])))())
                        k.op("tensor", (lambda b=b, s=(g == 0), e=(g == 7):
                                        (lambda t: t.matmul(
                                            b[0:1, 0:T3], ctx.ones_col[:, :],
                                            sq3[:, :], start=s, stop=e)))())
                    k.op("vector", (lambda b=b: (lambda v: v.tensor_copy(
                        srow0[0:1, :], b[0:1, 0:T3])))())
                    k.dma("sync", qkr[j:j + 1, :], srow0[0:1, :])
                for t0 in range(0, T3, 128):
                    bp = ctx.ps_aux
                    k.op("tensor", (lambda t0=t0: (lambda t: t.transpose(
                        bp[0:128, 0:R3], qkr[0:R3, t0:t0 + 128],
                        ctx.ident_f[0:R3, 0:R3])))())
                    k.op("vector", lambda v: v.tensor_copy(tmq[:, :],
                                                           bp[0:128, 0:R3]))
                    k.op("tensor", (lambda t0=t0: (lambda t: t.transpose(
                        bp[0:128, 0:R3], k2r[0:R3, t0:t0 + 128],
                        ctx.ident_f[0:R3, 0:R3])))())
                    k.op("vector", lambda v: v.tensor_copy(tmk[:, :],
                                                           bp[0:128, 0:R3]))
                    k.op("tensor", (lambda t0=t0: (lambda t: t.transpose(
                        bp[0:128, 0:1], q2r[0:1, t0:t0 + 128],
                        ctx.ident_f[0:1, 0:1])))())
                    k.op("vector", lambda v: v.tensor_copy(mncol[:, :],
                                                           bp[0:128, 0:1]))
                    k.op("vector", lambda v: v.tensor_scalar(
                        tm2[:, :], tmq[:, :], -2.0, None, ALU.mult))
                    k.op("vector", lambda v: v.tensor_add(tm2[:, :], tm2[:, :],
                                                          tmk[:, :]))
                    k.op("vector", lambda v: v.tensor_scalar(
                        tm2[:, :], tm2[:, :], mncol[:, :], None, ALU.add))
                    k.op("vector", lambda v: v.tensor_scalar(
                        tm2[:, :], tm2[:, :], 0.0, None, ALU.max))
                    k.op("vector", lambda v: v.tensor_reduce(
                        mncol[:, :], tm2[:, :], AX.X, ALU.min))
                    k.op("vector", (lambda ga=gamma: (lambda v: v.tensor_scalar(
                        smcol[:, :], mncol[:, :], ga, None, ALU.mult)))())
                    k.op("scalar", (lambda ga=gamma: (lambda s: s.activation(
                        tm2[:, :], tm2[:, :], AF.Exp, scale=-ga,
                        bias=smcol[:, :])))())
                    k.op("vector", lambda v: v.tensor_reduce(
                        mncol[:, :], tm2[:, :], AX.X, ALU.add))
                    k.op("vector", lambda v: v.reciprocal(mncol[:, :],
                                                          mncol[:, :]))
                    k.op("vector", lambda v: v.tensor_scalar(
                        tm2[:, :], tm2[:, :], mncol[:, :], None, ALU.mult))
                    k.op("tensor", (lambda t0=t0: (lambda t: t.transpose(
                        bp[0:R3, 0:128], tm2[0:128, 0:R3], ctx.ident_f[:, :])))())
                    k.op("vector", (lambda t0=t0: (lambda v: v.tensor_copy(
                        simr[0:R3, t0:t0 + 128], bp[0:R3, 0:128])))())
                k.dma("sync", f8[80:88, 0:T3], simr[0:R3, :])
                for g in range(8):
                    k.op("vector", (lambda g=g: (lambda v: v.memset(
                        accv[:, g * T3:(g + 1) * T3], 0.0)))())
                for j in range(R3):
                    k.dma("sync", srow[0:1, :], f8[80 + j:81 + j, 0:T3])
                    b = ctx.bank()
                    k.op("tensor", (lambda b=b: (lambda t: t.matmul(
                        b[:, 0:T3], ctx.ones_row[0:1, :], srow[0:1, :],
                        start=True, stop=True)))())
                    k.op("vector", (lambda b=b: (lambda v: v.tensor_copy(
                        sbc[:, :], b[:, 0:T3])))())
                    for g in range(8):
                        k.op("vector", (lambda j=j, g=g: (lambda v: v.tensor_mul(
                            sq3[:, :], vall[:, (g * R3 + j) * T3:(g * R3 + j + 1) * T3],
                            sbc[:, :])))())
                        k.op("vector", (lambda g=g: (lambda v: v.tensor_add(
                            accv[:, g * T3:(g + 1) * T3],
                            accv[:, g * T3:(g + 1) * T3], sq3[:, :])))())
                for g in range(8):
                    k.dma("sync", xr3[:, :],
                          xs_dram[rr * 1024 + g * 128:rr * 1024 + (g + 1) * 128, :])
                    k.op("vector", (lambda g=g: (lambda v: v.tensor_add(
                        sq3[:, :], xr3[:, :], accv[:, g * T3:(g + 1) * T3])))())
                    k.dma("sync", x_out[rr * 1024 + g * 128:rr * 1024 + (g + 1) * 128,
                                        :], sq3[:, :])


def l3_ffn(ctx, xs_dram, wgT, wuT, wdT, x_out, scratch):
    k = ctx.k
    HID = 3072
    y3 = scratch["l3y"]
    with contextlib.ExitStack() as S0:
        c127 = ctx.sb("c127", [128, 8], stack=S0)
        c0 = ctx.sb("c0", [128, 8], stack=S0)
        k.op("gpsimd", lambda g: g.memset(c127[:, :], 1.0 / 127.0))
        k.op("gpsimd", lambda g: g.memset(c0[:, :], 0.0))
        for rr in range(R3):
            with contextlib.ExitStack() as SL:
                ln_pass(ctx, xs_dram[rr * 1024:(rr + 1) * 1024, :],
                        y3[rr * 1024:(rr + 1) * 1024, :], SL, N=T3,
                        gw=c127, gb=c0)
        with contextlib.ExitStack() as S1:
            g_buf = ctx.sb("q_g", [128, 24 * T3], BF16, stack=S1)
            sil3 = ctx.sb("q_sil", [128, 512], F32, stack=S1)
            t3 = ctx.sb("q_t", [128, 512], F32, stack=S1)
            xr3 = ctx.sb("q_x", [128, 512], F32, stack=S1)
            at2 = lambda g, n0, nw: g_buf[:, g * T3 + n0:g * T3 + n0 + nw]
            for rr in range(R3):
                with contextlib.ExitStack() as SA:
                    act_sb, act_at = act_loader(
                        ctx, y3[rr * 1024:(rr + 1) * 1024, :], D, T3, stack=SA)

                    def evac_g(bA, bB, m0, mw, n0, nw):
                        k.op("scalar", (lambda bA=bA, mw=mw, nw=nw:
                                        (lambda s: s.activation(
                                            sil3[:, 0:nw], bA[0:mw, 0:nw],
                                            AF.Silu)))())
                        k.op("vector", (lambda bB=bB, mw=mw, nw=nw:
                                        (lambda v: v.tensor_mul(
                                            t3[:, 0:nw], sil3[:, 0:nw],
                                            bB[0:mw, 0:nw])))())
                        k.op("vector", (lambda m0=m0, n0=n0, nw=nw:
                                        (lambda v: v.tensor_scalar(
                                            g_buf[:, (m0 // 128) * T3 + n0:
                                                  (m0 // 128) * T3 + n0 + nw],
                                            t3[:, 0:nw], 1.0 / 127.0,
                                            None, ALU.mult)))())
                    gemm2(ctx, wgT, wuT, D, HID, act_at, T3, evac_g, stack=SA,
                          pre=True)
                    s1tok = k.last

                    def evac_d(b, m0, mw, n0, nw, rr=rr):
                        k.dma("sync", xr3[0:mw, 0:nw],
                              xs_dram[rr * 1024 + m0:rr * 1024 + m0 + mw,
                                      n0:n0 + nw])
                        k.op("vector", (lambda b=b, mw=mw, nw=nw:
                                        (lambda v: v.tensor_add(
                                            t3[:, 0:nw], xr3[0:mw, 0:nw],
                                            b[0:mw, 0:nw])))())
                        k.dma("sync", x_out[rr * 1024 + m0:rr * 1024 + m0 + mw,
                                            n0:n0 + nw], t3[0:mw, 0:nw])
                    gemm(ctx, wdT, HID, D, at2, T3, evac_d, stack=SA, pre=True,
                         act_tok=s1tok)



def build_fused(nc, ctx, ins, x_out, scratch):
    k = ctx.k
    fxA = nc.dram_tensor("fxA", [D, NT], F32).ap()
    fxB = nc.dram_tensor("fxB", [D, NT], F32).ap()
    ag_out = nc.dram_tensor("ag_out", [8 * D, NT], F32).ap()
    xsl = nc.dram_tensor("xsl", [8 * D, 512], F32).ap()
    xmid = nc.dram_tensor("xmid", [8 * D, 512], F32).ap()

    i = ins
    l0_attn(ctx, i["x"], i["wq"], i["wk"], i["wv"], i["wout"], fxA, scratch)
    l0_ffn(ctx, fxA, i["rwT"], i["ebias"], i["w1T"], i["w3T"], i["w2T"],
           i["sw1T"], i["sw3T"], i["sw2T"], fxB, scratch)
    l1_attn(ctx, fxB, i["wwT"], i["wrT"], i["wbT"], i["dec"], fxA, scratch)
    l1_ffn(ctx, fxA, i["ln1w"], i["ln1b"], i["ln2w"], i["ln2b"], i["wqkv"],
           i["bqkv"], i["wo"], i["bo"], i["gatew"], i["fc1T"], i["fc1b"],
           i["fc2T"], i["fc2b"], fxB, scratch)
    l2_attn(ctx, fxB, i["w_q"], i["w_k"], i["w_v"], i["w_b"], i["conv_q"],
            i["conv_k"], i["conv_v"],
            [i["fir1"], i["fir3"], i["fir7"], i["fir15"], i["fir31"]],
            i["gate_w1"], i["gate_b1"], i["gate_w2"], i["gate_b2"],
            i["norm_w"], i["w_o"], fxA, scratch)
    l2_ffn(ctx, fxA, i["gwT"], i["w1"], i["b1"], i["w2"], i["b2"], i["sw1"],
           i["sb1"], i["sw2"], i["sb2"], fxB, scratch)

    k.op("gpsimd", lambda g: g.collective_compute(
        "AllGather", ALU.bypass, replica_groups=[list(range(8))],
        ins=[fxB.opt() if hasattr(fxB, "opt") else fxB],
        outs=[ag_out.opt() if hasattr(ag_out, "opt") else ag_out]), inc=1)

    def do_slice(g):
        with g.register("rank") as rk, g.register("off") as off:
            g.cc_rank_ld(rk, replica_groups=[list(range(8))])
            g.reg_mul(off, rk, 512)
            offv = g.snap(off)
            ov = xsl.rearrange("(a p) m -> p a m", p=128)
            iv = ag_out.rearrange("(a p) m -> p a m", p=128)[:, :, bass.ds(offv, 512)]
            g.dma_start(out=ov, in_=iv).then_inc(k.sems["gpsimd"], 16)
        return None
    k.op("gpsimd", do_slice, inc=16, pre_inced=True)

    l3_attn(ctx, xsl, i["wqT"], i["bq"], i["wkT"], i["bk"], i["wvT"], i["bv"],
            xmid, scratch)
    l3_ffn(ctx, xmid, i["wgT"], i["wuT"], i["wdT"], x_out, scratch)


def _pg2(w):
    w = np.asarray(w, np.float32)
    K, M = w.shape
    kg, mg = K // 128, M // 128
    return np.ascontiguousarray(
        w.reshape(kg, 128, mg, 128).transpose(2, 1, 0, 3).reshape(mg * 128,
                                                                  kg * 128))


def _pgN(w):
    w = np.asarray(w, np.float32)
    if w.ndim == 2:
        return _pg2(w)
    return np.ascontiguousarray(np.stack([_pg2(w[i]) for i in range(w.shape[0])]))


_PRE_KEYS = {"wq", "wk", "wout", "w1T", "w3T", "w2T", "sw1T", "sw3T", "sw2T",
             "wwT", "wrT", "wqkv", "wo", "fc1T", "fc2T", "w_q", "w_k", "w_v",
             "w_o", "w1", "w2", "sw1", "sw2", "wqT", "wkT", "wvT", "wgT",
             "wuT", "wdT"}


def _prep_weights(ins):
    E = 8
    c128 = lambda v: np.ascontiguousarray(np.asarray(v).reshape(-1, 128).T)
    tr3 = lambda w: np.ascontiguousarray(np.asarray(w).transpose(0, 2, 1))
    T = lambda w: np.ascontiguousarray(np.asarray(w).T)
    w = {}
    w["l0a"] = {"wq": ins["L0_attn_Wq"], "wk": ins["L0_attn_Wk"],
                "wv": ins["L0_attn_Wv"], "wout": ins["L0_attn_Wout"]}
    w["l0f"] = {"rwT": T(ins["L0_ffn_router_w"]),
                "ebias": np.ascontiguousarray(
                    np.asarray(ins["L0_ffn_expert_bias"]).reshape(E, 1)),
                "w1T": tr3(ins["L0_ffn_w1"]), "w3T": tr3(ins["L0_ffn_w3"]),
                "w2T": tr3(ins["L0_ffn_w2"]), "sw1T": tr3(ins["L0_ffn_sw1"]),
                "sw3T": tr3(ins["L0_ffn_sw3"]), "sw2T": tr3(ins["L0_ffn_sw2"])}
    w["l1a"] = {"wwT": T(ins["L1_attn_W_write"]), "wrT": T(ins["L1_attn_W_read"]),
                "wbT": T(ins["L1_attn_W_beta"]),
                "dec": np.ascontiguousarray(
                    np.asarray(ins["L1_attn_decay"]).reshape(8, 1))}
    w["l1f"] = {"ln1w": c128(ins["L1_ffn_ln1_w"]), "ln1b": c128(ins["L1_ffn_ln1_b"]),
                "ln2w": c128(ins["L1_ffn_ln2_w"]), "ln2b": c128(ins["L1_ffn_ln2_b"]),
                "wqkv": ins["L1_ffn_Wqkv"], "bqkv": c128(ins["L1_ffn_bqkv"]),
                "wo": ins["L1_ffn_Wo"], "bo": c128(ins["L1_ffn_bo"]),
                "gatew": ins["L1_ffn_gate_W"],
                "fc1T": tr3(ins["L1_ffn_fc1_w"]),
                "fc1b": np.ascontiguousarray(
                    np.asarray(ins["L1_ffn_fc1_b"]).reshape(E, 32, 128)
                    .transpose(0, 2, 1)),
                "fc2T": tr3(ins["L1_ffn_fc2_w"]),
                "fc2b": np.ascontiguousarray(
                    np.asarray(ins["L1_ffn_fc2_b"]).reshape(E, 8, 128)
                    .transpose(0, 2, 1))}
    w["l2a"] = {"w_q": ins["L2_attn_w_q"], "w_k": ins["L2_attn_w_k"],
                "w_v": ins["L2_attn_w_v"], "w_b": ins["L2_attn_w_b"],
                "conv_q": ins["L2_attn_conv_q_w"], "conv_k": ins["L2_attn_conv_k_w"],
                "conv_v": ins["L2_attn_conv_v_w"],
                "fir1": ins["L2_attn_fir1"], "fir3": ins["L2_attn_fir3"],
                "fir7": ins["L2_attn_fir7"], "fir15": ins["L2_attn_fir15"],
                "fir31": ins["L2_attn_fir31"],
                "gate_w1": ins["L2_attn_gate_w1"],
                "gate_b1": np.ascontiguousarray(
                    np.asarray(ins["L2_attn_gate_b1"]).reshape(16, 128).T),
                "gate_w2": ins["L2_attn_gate_w2"],
                "gate_b2": np.ascontiguousarray(
                    np.asarray(ins["L2_attn_gate_b2"]).reshape(28, 1)),
                "norm_w": np.ascontiguousarray(
                    np.asarray(ins["L2_attn_norm_w"]).reshape(2, 128).T),
                "w_o": ins["L2_attn_w_o"]}
    w["l2f"] = {"gwT": T(ins["L2_ffn_gate_w"]), "w1": ins["L2_ffn_w1"],
                "b1": np.ascontiguousarray(
                    np.asarray(ins["L2_ffn_b1"]).reshape(2, 32, 128)
                    .transpose(0, 2, 1)),
                "w2": ins["L2_ffn_w2"],
                "b2": np.ascontiguousarray(
                    np.asarray(ins["L2_ffn_b2"]).reshape(2, 8, 128)
                    .transpose(0, 2, 1)),
                "sw1": ins["L2_ffn_sw1"], "sb1": c128(ins["L2_ffn_sb1"]),
                "sw2": ins["L2_ffn_sw2"], "sb2": c128(ins["L2_ffn_sb2"])}
    w["l3"] = {"wqT": T(ins["L3_attn_Wq"]), "bq": c128(ins["L3_attn_bq"]),
               "wkT": T(ins["L3_attn_Wk"]), "bk": c128(ins["L3_attn_bk"]),
               "wvT": T(ins["L3_attn_Wv"]), "bv": c128(ins["L3_attn_bv"]),
               "wgT": T(ins["L3_ffn_w_gate"]), "wuT": T(ins["L3_ffn_w_up"]),
               "wdT": T(ins["L3_ffn_w_down"])}
    for d in w.values():
        for kk in d:
            a = np.asarray(d[kk])
            if a.dtype not in (np.float32, np.int32):
                a = a.astype(np.float32)
            if kk in _PRE_KEYS:
                a = _pgN(a)
            d[kk] = np.ascontiguousarray(a)
    return w


def _launch(builder, ins_np, n_cores=8, out_shape=(D, NT)):
    nc = bass.Bass("TRN2", target_bir_lowering=False)
    ins_ap = {}
    for nm, arr in ins_np[0].items():
        dt = I32 if arr.dtype == np.int32 else F32
        ins_ap[nm] = nc.dram_tensor(nm, list(arr.shape), dt,
                                    kind="ExternalInput").ap()
    x_out = nc.dram_tensor("x_out", list(out_shape), F32,
                           kind="ExternalOutput").ap()
    scratch = make_scratch(nc)
    ctx = Ctx(nc)
    with ctx.st:
        ctx.setup()
        builder(ctx, ins_ap, x_out, scratch)
        ctx.k.op("sync", lambda s: None, after=ctx.k.all_now(), inc=1)
        with nc.Block() as block:
            ctx.k.emit(block)
    res = run_bass_kernel_spmd(nc, ins_np, core_ids=list(range(n_cores)))
    return [np.asarray(res.results[c]["x_out"]) for c in range(n_cores)]


def kernel(**inputs):
    _ = inputs.get("L0_attn_mask")
    w = _prep_weights(inputs)
    flat = {}
    for d in w.values():
        flat.update(d)
    x0 = np.asarray(inputs["L0_attn_x"], np.float32)
    xs = [np.ascontiguousarray(x0[c].T) for c in range(8)]

    nc = bass.Bass("TRN2", target_bir_lowering=False)
    ins_ap = {}
    for nm, arr in flat.items():
        dt = I32 if arr.dtype == np.int32 else F32
        ins_ap[nm] = nc.dram_tensor(nm, list(arr.shape), dt,
                                    kind="ExternalInput").ap()
    ins_ap["x"] = nc.dram_tensor("x", [D, NT], F32, kind="ExternalInput").ap()
    x_out = nc.dram_tensor("x_out", [8 * D, 512], F32,
                           kind="ExternalOutput").ap()
    scratch = make_scratch(nc)
    ctx = Ctx(nc)
    with ctx.st:
        ctx.setup()
        build_fused(nc, ctx, ins_ap, x_out, scratch)
        ctx.k.op("sync", lambda s: None, after=ctx.k.all_now(), inc=1)
        with nc.Block() as block:
            ctx.k.emit(block)
    in_maps = [dict(flat, x=xs[c]) for c in range(8)]
    res = run_bass_kernel_spmd(nc, in_maps, core_ids=list(range(8)))
    out = np.empty((8, 4096, 1024), np.float32)
    for c in range(8):
        oc = np.asarray(res.results[c]["x_out"]).reshape(8, D, 512)
        out[:, c * 512:(c + 1) * 512, :] = oc.transpose(0, 2, 1)
    return out
